# version 48
# baseline (speedup 1.0000x reference)
import math
import os
import contextlib
import numpy as np
import concourse.bass as bass
import concourse.mybir as mybir
from concourse.bass_utils import run_bass_kernel_spmd

F32 = mybir.dt.float32
BF16 = mybir.dt.bfloat16
I32 = mybir.dt.int32
AF = mybir.ActivationFunctionType
ALU = mybir.AluOpType
AX = mybir.AxisListType
IOA = bass.IndirectOffsetOnAxis

NCORES = 8
SELF_WAITS = os.environ.get("KSELFW", "1") == "1"
KWARM = int(os.environ.get("KWARM", "3"))
SEQ = 4
T = 2048
D = 1024
H = 16
HD = 64
ROPE = 32
LAT = 128
E = 8
CAP = 769
CPAD = 896
NSLOT = E * CPAD
ZROW = NSLOT
NT = T // 128
LN8 = math.log(8.0)

C_IDENT = 0
C_L128 = 128
C_L16 = 256
C_IOTA8 = 272
C_SELQ = 280
C_SELK = 282
C_BCQ = 286
C_BCK = 414
C_ONESROW = 542
C_ONESCOL = 670
C_TRASH = 671
C_ONESB = 672
C_IMM = 704
IMM_VALS = [0.0, 1.0, -0.5, 1e-5, 1e-6, -LN8, 1.0 / 64.0, 1.0 / 32.0, -1.0]
CF = 720
CB_COLS = 897

AB_COLS = 43264
A_LNT = 0
A_V = 0
A_PR = 8448
A_QP = 16384
A_KP = 24576
A_KVT = 32768
A_ATT = 34816
FB_WE = 0
FB_INT = 16384
FB_G0 = 30720
FB_G1 = 34816
AB_NEED = max(A_ATT + 8192, FB_G1 + 4096)

AF_COLS = 5376
AQ_SQS = 0
AQ_SQS2 = 512
AQ_RSTD = 1024
AQ_BC = 1536
AQ_DENF = 2048
AQ_RECB = 4096
FF_EO = 0
FF_LGT = 0
FF_COMB = 2048
FF_LTOK = 4096
FF_R128 = 4224
FF_R16 = 4992
FF_SR = 5184
FF_C16 = 5312
FF_CB = 5320
assert FF_CB + 8 <= AF_COLS


class Sem:
    def __init__(self, h):
        self.h = h
        self.n = 0


class KB:

    def __init__(self, nc):
        self.nc = nc
        self.th = {k: [] for k in ("tensor", "vector", "scalar", "gpsimd", "sync")}
        self.sems = []
        self._scope_stack = []
        self.eng_sem = {}

    def sem(self, es, name):
        s = Sem(es.enter_context(self.nc.semaphore(name)))
        self.sems.append(s)
        return s

    def em(self, eng, fn):
        self.th[eng].append(fn)

    def w(self, eng, sem, val):
        if val <= 0:
            return
        self.em(eng, lambda e, s=sem.h, v=val: e.wait_ge(s, v))

    def op(self, eng, fn, sem=None, k=1):
        own = self.eng_sem.get(eng)
        if own is not None and not SELF_WAITS:
            if sem is None:
                sem = own
            own = None
        if own is not None:
            if sem is None:
                sem = own
            self.w(eng, own, own.n)
        if sem is None:
            self.em(eng, fn)
            return None
        sem.n += k
        v = sem.n
        self.em(eng, lambda e, f=fn, s=sem.h, kk=k: f(e).then_inc(s, kk))
        return v

    def dma(self, eng, out, in_, sem, **kw):
        return self.op(
            eng, lambda e, o=out, i=in_, k2=dict(kw): e.dma_start(out=o, in_=i, **k2),
            sem, 16)

    def scope(self, name):
        ids = {}
        for eng in self.th:
            def _enter(e, n=name, eng=eng, ids=ids):
                ids[eng] = self.nc.enter_named_scope(n, False)[0]
            self.em(eng, _enter)
        self._scope_stack.append((name, ids))

    def escope(self):
        name, ids = self._scope_stack.pop()
        for eng in self.th:
            self.em(eng, lambda e, n=name, eng=eng, ids=ids:
                    self.nc.leave_named_scope(n, ids[eng], False))

    def warm(self, tgt, lhsT, rhs, n=2):
        if KWARM >= 3:
            return
        n = max(1, n // (1 + (KWARM > 0)))
        w = 512 // (1 + (KWARM > 1))
        for _ in range(n):
            self.op("tensor", lambda e, t=tgt, l=lhsT, r=rhs, w=w: e.matmul(
                t[:, 0:w], l, r[:, 0:w], start=True, stop=True,
                skip_group_check=True), self.eng_warm)

    def barrier(self):
        snap = [(s, s.n) for s in self.sems if s.n > 0]
        for eng in self.th:
            for s, v in snap:
                self.w(eng, s, v)


def build_program():
    nc = bass.Bass("TRN2", target_bir_lowering=False)

    x_d = nc.dram_tensor("x", [SEQ * T, D], F32, kind="ExternalInput").ap()
    wq_d = nc.dram_tensor("wq", [D, D], F32, kind="ExternalInput").ap()
    wkv_d = nc.dram_tensor("wkv", [D, LAT], F32, kind="ExternalInput").ap()
    wk_d = nc.dram_tensor("wk", [LAT, H * ROPE], F32, kind="ExternalInput").ap()
    wv_d = nc.dram_tensor("wv", [LAT, H * ROPE], F32, kind="ExternalInput").ap()
    wop_d = nc.dram_tensor("wop", [H * ROPE, D], F32, kind="ExternalInput").ap()
    wrt_d = nc.dram_tensor("wrt", [D, E], F32, kind="ExternalInput").ap()
    we_d = nc.dram_tensor("we", [E, D, D], F32, kind="ExternalInput").ap()
    qg_d = nc.dram_tensor("qg", [HD], F32, kind="ExternalInput").ap()
    kg_d = nc.dram_tensor("kg", [ROPE], F32, kind="ExternalInput").ap()
    qgc_d = nc.dram_tensor("qgc", [128, 1], F32, kind="ExternalInput").ap()
    kgc_d = nc.dram_tensor("kgc", [128, 1], F32, kind="ExternalInput").ap()
    cstf_d = nc.dram_tensor("cstf", [128, CF], F32, kind="ExternalInput").ap()
    cstb_d = nc.dram_tensor("cstb", [128, CB_COLS], F32, kind="ExternalInput").ap()

    out_d = nc.dram_tensor("out_x", [SEQ * T, D], F32, kind="ExternalOutput").ap()

    htok_d = nc.dram_tensor("htok", [T, D], BF16, kind="Internal").ap()
    disp_d = nc.dram_tensor("disp", [NSLOT + 128, 1024], BF16, kind="Internal").ap()
    eout_d = nc.dram_tensor("eout", [NSLOT + 128, 1024], BF16, kind="Internal").ap()
    webf_d = nc.dram_tensor("webf", [E, D, D], BF16, kind="Internal").ap()
    csa_d = nc.dram_tensor("csa", [128], F32, kind="Internal").ap()
    csb_d = nc.dram_tensor("csb", [128], F32, kind="Internal").ap()

    es = contextlib.ExitStack()
    with es:
        sb = lambda name, shape, dt: es.enter_context(nc.sbuf_tensor(name, shape, dt))
        x_sb = sb("x_sb", [128, NT * D], F32)
        hst = sb("hst", [128, 2 * D], BF16)
        cstf = sb("cstf_sb", [128, CF], F32)
        cstb = sb("cstb_sb", [128, CB_COLS], BF16)
        wq_sb = sb("wq_sb", [128, 8 * D], BF16)
        wkv_sb = sb("wkv_sb", [128, 8 * LAT], BF16)
        wk_sb = sb("wk_sb", [128, 512], BF16)
        wv_sb = sb("wv_sb", [128, 512], BF16)
        wop_sb = sb("wop_sb", [128, 4 * D], BF16)
        wrt_sb = sb("wrt_sb", [128, 64], BF16)
        qg_col = sb("qg_col", [128, 1], F32)
        kg_col = sb("kg_col", [128, 1], F32)
        qg_row = sb("qg_row", [1, HD], F32)
        kg_row = sb("kg_row", [1, ROPE], F32)
        negc = sb("negc", [128, 1], F32)
        c0t = sb("c0t", [1, 4], F32)
        mvraw = sb("mvraw", [128, 12], F32)
        mv_sb = sb("mv_sb", [128, 2 * NT], F32)
        rstd_ln = sb("rstd_ln", [128, NT], F32)
        bias_ln = sb("bias_ln", [128, NT], F32)
        zero_sb = sb("zero_sb", [128, 512], BF16)
        dsti = sb("dsti", [128, 32], I32)
        ab = sb("arena_bf", [128, AB_NEED], BF16)
        af = sb("arena_f", [128, AF_COLS], F32)
        psA = es.enter_context(nc.psum_tensor("psA", [128, 2048], F32))
        psB = es.enter_context(nc.psum_tensor("psB", [128, 2048], F32))
        ps = [psA[:, 512 * i:512 * (i + 1)] for i in range(4)] + \
             [psB[:, 512 * i:512 * (i + 1)] for i in range(4)]

        kb = KB(nc)
        dSet = kb.sem(es, "dSet")
        dSetG = kb.sem(es, "dSetG")
        dZ = kb.sem(es, "dZ")
        dX = kb.sem(es, "dX")
        dH = [kb.sem(es, "dH0"), kb.sem(es, "dH1")]
        dTr = [kb.sem(es, "dTr0"), kb.sem(es, "dTr1")]
        dInt = [kb.sem(es, "dInt0"), kb.sem(es, "dInt1")]
        dWe = [kb.sem(es, "dWe0"), kb.sem(es, "dWe1")]
        dWeG = [kb.sem(es, "dWeG0"), kb.sem(es, "dWeG1")]
        dWb = [kb.sem(es, "dWb0"), kb.sem(es, "dWb1")]
        dEo = [kb.sem(es, "dEo0"), kb.sem(es, "dEo1")]
        dSc = [kb.sem(es, "dSc0"), kb.sem(es, "dSc1")]
        dGa = [kb.sem(es, "dGa0"), kb.sem(es, "dGa1")]
        dCs = kb.sem(es, "dCs")
        dSg = [kb.sem(es, f"dSg{i}") for i in range(4)]
        dSt2 = [kb.sem(es, "dSt0"), kb.sem(es, "dSt1")]
        cT = kb.sem(es, "cT")
        cV = kb.sem(es, "cV")
        cA = kb.sem(es, "cA")
        cP = kb.sem(es, "cP")
        kb.eng_sem = {"vector": cV, "scalar": cA, "gpsimd": cP}
        kb.eng_warm = cT

        for j, v in enumerate(IMM_VALS):
            nc.const_aps.register(F32, v, cstf[:, C_IMM + j:C_IMM + j + 1])

        ident = cstf[:, C_IDENT:C_IDENT + 128]
        L128 = cstf[:, C_L128:C_L128 + 128]
        L16 = cstf[0:16, C_L16:C_L16 + 16]
        iota8 = cstf[:, C_IOTA8:C_IOTA8 + 8]
        selq = cstf[:, C_SELQ:C_SELQ + 2]
        selk = cstf[:, C_SELK:C_SELK + 4]
        bcq = cstf[0:2, C_BCQ:C_BCQ + 128]
        bck = cstf[0:4, C_BCK:C_BCK + 128]
        onesrow = cstf[0:1, C_ONESROW:C_ONESROW + 128]
        onesb = cstf[:, C_ONESB:C_ONESB + 32]
        ones128 = cstf[:, C_ONESCOL:C_ONESCOL + 1].broadcast_to([128, 128])
        onescol_b = cstb[:, 896:897]
        onescol = cstf[:, C_ONESCOL:C_ONESCOL + 1]
        trash = cstf[:, C_TRASH:C_TRASH + 1]

        v_cst = kb.dma("sync", cstf[:], cstf_d[:, :], dSet)
        kb.dma("gpsimd", cstb[:], cstb_d[:, :], dSetG)
        for dc in range(8):
            kb.dma("gpsimd", wq_sb[:, D * dc:D * (dc + 1)],
                   wq_d[128 * dc:128 * (dc + 1), :], dSetG)
            kb.dma("gpsimd", wkv_sb[:, LAT * dc:LAT * (dc + 1)],
                   wkv_d[128 * dc:128 * (dc + 1), :], dSetG)
        kb.dma("gpsimd", wk_sb[:], wk_d[:, :], dSetG)
        kb.dma("gpsimd", wv_sb[:], wv_d[:, :], dSetG)
        for kc in range(4):
            kb.dma("gpsimd", wop_sb[:, D * kc:D * (kc + 1)],
                   wop_d[128 * kc:128 * (kc + 1), :], dSetG)
        for dc in range(8):
            kb.dma("gpsimd", wrt_sb[:, 8 * dc:8 * (dc + 1)],
                   wrt_d[128 * dc:128 * (dc + 1), :], dSetG)
        kb.dma("sync", qg_col[:], qgc_d[:, :], dSet)
        kb.dma("sync", kg_col[:], kgc_d[:, :], dSet)
        kb.dma("sync", qg_row[:], qg_d.rearrange("(r j) -> r j", r=1), dSet)
        v_kgr = kb.dma("sync", kg_row[:], kg_d.rearrange("(r j) -> r j", r=1), dSet)

        kb.op("vector", lambda e: e.memset(zero_sb[:], 0.0), cV)
        v3d = ab[:, A_V:A_V + 8448].rearrange("p (t h d) -> p t h d", h=H, d=33)

        for b in (disp_d, eout_d):
            kb.w("gpsimd", cV, 1)
            for hf in range(2):
                kb.dma("gpsimd", b[ZROW:ZROW + 128, 512 * hf:512 * (hf + 1)],
                       zero_sb[:], dZ)

        kb.w("vector", dSet, dSet.n)
        kb.w("vector", dSetG, dSetG.n)
        kb.op("vector", lambda e: e.tensor_reduce(
            c0t[:, 0:1], qg_row[:, 0:ROPE], axis=AX.X, op=ALU.max,
            apply_absolute_value=True), cV)
        kb.op("vector", lambda e: e.tensor_reduce(
            c0t[:, 1:2], kg_row[:], axis=AX.X, op=ALU.max,
            apply_absolute_value=True), cV)
        kb.op("vector", lambda e: e.tensor_tensor(
            c0t[:, 2:3], c0t[:, 0:1], c0t[:, 1:2], op=ALU.mult), cV)
        v_c0 = kb.op("vector", lambda e: e.tensor_scalar(
            c0t[:, 3:4], c0t[:, 2:3], -math.sqrt(32.0), None, op0=ALU.mult), cV)
        kb.w("tensor", cV, v_c0)
        kb.w("tensor", dSet, dSet.n)
        v_ncps = kb.op("tensor", lambda e: e.matmul(
            ps[0][0:128, 0:1], onesrow, c0t[:, 3:4], start=True, stop=True), cT)
        kb.w("vector", cT, v_ncps)
        kb.op("vector", lambda e: e.tensor_copy(negc[:], ps[0][0:128, 0:1]), cV)

        kb.barrier()

        state = dict(first_ffn=True)


        def emit_stats(i):
            xt = x_sb[:, D * i:D * (i + 1)]
            kb.op("vector", lambda e, xt=xt: e.bn_stats(
                mvraw[:, 0:6], xt[:, 0:512]), cV)
            kb.op("vector", lambda e, xt=xt: e.bn_stats(
                mvraw[:, 6:12], xt[:, 512:1024]), cV)
            kb.op("vector", lambda e, i=i: e.bn_aggr(
                mv_sb[:, 2 * i:2 * i + 2], mvraw[:]), cV)

        def emit_ln(s, skip_stats=False):
            if not skip_stats:
                kb.w("vector", dX, dX.n)
                for i in range(NT):
                    emit_stats(i)
            v_bn = cV.n
            var_view = mv_sb[:].rearrange("p (t two) -> p two t", two=2)[:, 1, :]
            mean_view = mv_sb[:].rearrange("p (t two) -> p two t", two=2)[:, 0, :]
            kb.w("scalar", cV, v_bn)
            kb.op("scalar", lambda e: e.activation(
                bias_ln[:], var_view, AF.Ln, bias=1e-5, scale=1.0), cA)
            v_l = kb.op("scalar", lambda e: e.activation(
                rstd_ln[:], bias_ln[:], AF.Exp, bias=0.0, scale=-0.5), cA)
            kb.w("vector", cA, v_l)
            v_b = kb.op("vector", lambda e: e.scalar_tensor_tensor(
                bias_ln[:], mean_view, -1.0, rstd_ln[:],
                op0=ALU.mult, op1=ALU.mult), cV)
            kb.w("scalar", cV, v_b)
            kb.w("tensor", cV, cV.n)
            for i in range(NT):
                par = i % 2
                kb.w("scalar", dH[par], dH[par].n)
                v_h = kb.op("scalar", lambda e, i=i, par=par: e.activation(
                    hst[:, D * par:D * (par + 1)], x_sb[:, D * i:D * (i + 1)],
                    AF.Identity, bias=bias_ln[:, i:i + 1],
                    scale=rstd_ln[:, i:i + 1]), cA)
                kb.w("sync", cA, v_h)
                kb.dma("sync", htok_d[128 * i:128 * (i + 1), :],
                       hst[:, D * par:D * (par + 1)], dH[par])
                if i % 4 == 3:
                    kb.w("tensor", cA, v_h)
                    kb.warm(ps[7][0:64, :], wq_sb[:, 0:64], wq_sb[:, 0:512], 4)

        def emit_ffn(s, skip_stats=False, tail_stats=False):
            emit_ln(s, skip_stats)
            lgT = af[:, FF_LGT:FF_LGT + 2048]
            ltok = af[:, FF_LTOK:FF_LTOK + 128]
            R = lambda j: af[:, FF_R128 + 128 * j:FF_R128 + 128 * (j + 1)]
            oh0, oh1, ohsum, prefix, pos_all, tmp = (R(j) for j in range(6))
            S = lambda j: af[:, FF_R16 + 16 * j:FF_R16 + 16 * (j + 1)]
            m0, m1, idx0, idx1, pos0, pos1, w0, w1, drop, d0f, d1f, _sp = (
                S(j) for j in range(12))
            sr = af[0:1, FF_SR:FF_SR + 128]
            c16 = af[0:16, FF_C16:FF_C16 + 8]
            cb16 = af[0:16, FF_CB:FF_CB + 8]

            kb.w("tensor", cV, state.get("oproj_cV", 0))
            lnTc = ab[:, A_LNT:A_LNT + 4 * T]
            kb.w("sync", cT, state.get("core_cT", 0))
            kb.w("sync", cV, state.get("core_cV", 0))
            v_mm_prev = [0, 0, 0, 0]
            tr_hist = [0, 0]
            for dc in range(8):
                par = dc % 4
                kb.w("sync", dH[0], dH[0].n)
                kb.w("sync", dH[1], dH[1].n)
                kb.w("sync", cT, v_mm_prev[par])
                kb.w("sync", dTr[dc % 2], tr_hist[dc % 2])
                tr_hist[dc % 2] = dTr[dc % 2].n + 16
                kb.dma("sync", lnTc[:, T * par:T * (par + 1)],
                       htok_d[:, 128 * dc:128 * (dc + 1)], dTr[dc % 2], transpose=True)
                kb.w("tensor", dTr[dc % 2], dTr[dc % 2].n)
                kb.warm(ps[7][0:64, :], wq_sb[:, 0:64], wq_sb[:, 0:512], 1)
                for c in range(4):
                    v = kb.op("tensor", lambda e, dc=dc, c=c, par=par: e.matmul(
                        ps[c][0:8, :],
                        wrt_sb[:, 8 * dc:8 * (dc + 1)],
                        lnTc[:, T * par + 512 * c:T * par + 512 * (c + 1)],
                        start=(dc == 0), stop=(dc == 7)), cT)
                v_mm_prev[par] = v
            v_lgmm = cT.n
            kb.w("vector", cT, v_lgmm)
            for c in range(4):
                kb.op("vector", lambda e, c=c: e.tensor_copy(
                    lgT[0:8, 512 * c:512 * (c + 1)], ps[c][0:8, :]), cV)
            v_lgev = cV.n

            stg = ab[:, 16384:16384 + 4096]

            def stga(i):
                return stg[:, D * (i % 4):D * (i % 4 + 1)]

            kb.w("sync", dH[0], dH[0].n)
            kb.w("sync", dH[1], dH[1].n)
            stga_dma = [0, 0, 0, 0]
            stg_sem = dSg
            for i in range(4):
                kb.dma("sync", stga(i), htok_d[128 * i:128 * (i + 1), :],
                       stg_sem[i])
                stga_dma[i] = stg_sem[i].n

            l3 = ltok.rearrange("p (t e) -> p t e", e=8)
            o3 = lambda a_: a_.rearrange("p (t e) -> p t e", e=8)
            i3h = iota8.unsqueeze(1).broadcast_to([128, 8, 8])
            scat_hist = [0, 0]
            kb.w("tensor", cV, v_lgev)
            for half in range(2):
                hs = slice(8 * half, 8 * half + 8)
                cs = slice(64 * half, 64 * half + 64)
                for i in range(8 * half, 8 * half + 8):
                    kb.op("tensor", lambda e, i=i: e.transpose(
                        ps[4][:, 8 * i:8 * (i + 1)],
                        lgT[0:8, 128 * i:128 * (i + 1)], ident[0:8, 0:8]), cT)
                v_lt = cT.n
                kb.w("vector", cT, v_lt)
                kb.op("vector", lambda e, cs=cs: e.tensor_copy(
                    ltok[:, cs], ps[4][:, cs]), cV)
                l3h = l3[:, hs, :]
                mb = lambda a_: a_[:, hs].unsqueeze(2).broadcast_to([128, 8, 8])
                kb.op("vector", lambda e, l3h=l3h, hs=hs: e.tensor_reduce(
                    m0[:, hs], l3h, axis=AX.X, op=ALU.max), cV)
                kb.op("vector", lambda e, l3h=l3h, hs=hs, mbv=mb(m0): e.tensor_tensor(
                    o3(oh0)[:, hs, :], l3h, mbv, op=ALU.is_ge), cV)
                kb.op("vector", lambda e, hs=hs, cs=cs: e.scalar_tensor_tensor(
                    tmp[:, cs], oh0[:, cs], -1e30, ltok[:, cs],
                    op0=ALU.mult, op1=ALU.add), cV)
                kb.op("vector", lambda e, hs=hs, cs=cs: e.tensor_reduce(
                    m1[:, hs], o3(tmp)[:, hs, :], axis=AX.X, op=ALU.max), cV)
                kb.op("vector", lambda e, hs=hs, mbv=mb(m1), cs=cs: e.tensor_tensor(
                    o3(oh1)[:, hs, :], o3(tmp)[:, hs, :], mbv, op=ALU.is_ge), cV)
                kb.op("vector", lambda e, hs=hs: e.tensor_tensor(
                    o3(tmp)[:, hs, :], o3(oh0)[:, hs, :], i3h, op=ALU.mult), cV)
                kb.op("vector", lambda e, hs=hs: e.tensor_reduce(
                    idx0[:, hs], o3(tmp)[:, hs, :], axis=AX.X, op=ALU.add), cV)
                kb.op("vector", lambda e, hs=hs: e.tensor_tensor(
                    o3(tmp)[:, hs, :], o3(oh1)[:, hs, :], i3h, op=ALU.mult), cV)
                kb.op("vector", lambda e, hs=hs: e.tensor_reduce(
                    idx1[:, hs], o3(tmp)[:, hs, :], axis=AX.X, op=ALU.add), cV)
                v_m = kb.op("vector", lambda e, hs=hs: e.tensor_tensor(
                    w1[:, hs], m1[:, hs], m0[:, hs], op=ALU.subtract), cV)
                kb.w("scalar", cV, v_m)
                v_e1 = kb.op("scalar", lambda e, hs=hs: e.activation(
                    w1[:, hs], w1[:, hs], AF.Exp, bias=0.0, scale=1.0), cA)
                kb.w("vector", cA, v_e1)
                kb.op("vector", lambda e, hs=hs: e.tensor_scalar(
                    drop[:, hs], w1[:, hs], 1.0, None, op0=ALU.add), cV)
                kb.op("vector", lambda e, hs=hs: e.reciprocal(
                    w0[:, hs], drop[:, hs]), cV)
                kb.op("vector", lambda e, hs=hs: e.tensor_tensor(
                    w1[:, hs], w1[:, hs], w0[:, hs], op=ALU.mult), cV)
                kb.op("vector", lambda e, hs=hs: e.tensor_scalar(
                    w1[:, hs], w1[:, hs], 0.5, None, op0=ALU.mult), cV)
                kb.op("vector", lambda e, hs=hs: e.tensor_scalar(
                    w0[:, hs], w0[:, hs], 0.5, None, op0=ALU.mult), cV)
                v_ohs = kb.op("vector", lambda e, hs=hs, cs=cs: e.tensor_tensor(
                    ohsum[:, cs], oh0[:, cs], oh1[:, cs], op=ALU.add), cV)
                kb.w("tensor", cV, v_ohs)
                pbank = ps[5 + half]
                for cch in range(8 * half, 8 * half + 8):
                    kb.op("tensor", lambda e, cch=cch, pbank=pbank, half=half: e.matmul(
                        pbank[:, 8 * (cch - 8 * half):8 * (cch - 8 * half) + 8], L128,
                        ohsum[:, 8 * cch:8 * (cch + 1)],
                        start=(cch == 8 * half), stop=False,
                        skip_group_check=True), cT)
                v_csum = cT.n
                lo = 8 * half
                for cch in range(0, 8 * half + 7):
                    tgt0 = max(cch + 1, lo)
                    rep = 8 * half + 8 - tgt0
                    if rep <= 0:
                        continue
                    rhs_rep = ohsum[:, 8 * cch:8 * (cch + 1)].unsqueeze(1) \
                        .broadcast_to([128, rep, 8])
                    v_csum = kb.op("tensor", lambda e, rhs_rep=rhs_rep, rep=rep,
                                   tgt0=tgt0, pbank=pbank, lo=lo: e.matmul(
                        pbank[:, 8 * (tgt0 - lo):8 * (tgt0 - lo) + 8 * rep],
                        ones128, rhs_rep,
                        start=False, stop=(tgt0 == lo and rep == 8 and False),
                        skip_group_check=True), cT)
                kb.w("vector", cT, v_csum)
                kb.op("vector", lambda e, cs=cs, pbank=pbank: e.tensor_copy(
                    pos_all[:, cs], pbank[:, 0:64]), cV)
                for posx, idxx, dfx, col, ohx in (
                        (pos0, idx0, d0f, 0, oh0), (pos1, idx1, d1f, 16, oh1)):
                    kb.op("vector", lambda e, hs=hs, cs=cs, ohx=ohx: e.tensor_tensor(
                        o3(tmp)[:, hs, :], o3(ohx)[:, hs, :], o3(pos_all)[:, hs, :],
                        op=ALU.mult), cV)
                    kb.op("vector", lambda e, p=posx, hs=hs: e.tensor_reduce(
                        p[:, hs], o3(tmp)[:, hs, :], axis=AX.X, op=ALU.add), cV)
                    kb.op("vector", lambda e, p=posx, hs=hs: e.tensor_scalar(
                        drop[:, hs], p[:, hs], float(CAP) - 0.5, None,
                        op0=ALU.is_gt), cV)
                    kb.op("vector", lambda e, p=posx, ix=idxx, d=dfx, hs=hs:
                          e.scalar_tensor_tensor(
                        d[:, hs], ix[:, hs], float(CPAD), p[:, hs],
                        op0=ALU.mult, op1=ALU.add), cV)
                    kb.op("vector", lambda e, d=dfx, hs=hs: e.tensor_scalar(
                        _sp[:, hs], d[:, hs], -1.0, float(ZROW),
                        op0=ALU.mult, op1=ALU.add), cV)
                    kb.op("vector", lambda e, hs=hs: e.tensor_tensor(
                        _sp[:, hs], _sp[:, hs], drop[:, hs], op=ALU.mult), cV)
                    kb.op("vector", lambda e, d=dfx, hs=hs: e.tensor_tensor(
                        _sp[:, hs], _sp[:, hs], d[:, hs], op=ALU.add), cV)
                    v_route = kb.op("vector", lambda e, col=col, hs=hs, half=half:
                                    e.tensor_copy(
                        dsti[:, col + 8 * half:col + 8 * half + 8], _sp[:, hs]), cV)
                kb.w("gpsimd", cV, v_route)
                for i in range(8 * half, 8 * half + 8):
                    sl = i % 4
                    kb.w("gpsimd", stg_sem[sl], stga_dma[sl])
                    kb.w("gpsimd", dSc[i % 2], scat_hist[i % 2])
                    scat_hist[i % 2] = dSc[i % 2].n + 32
                    kb.w("tensor", dSc[i % 2], dSc[i % 2].n)
                    kb.warm(ps[7][0:64, :], wq_sb[:, 0:64], wq_sb[:, 0:512], 3)
                    for kk in range(2):
                        kb.op("gpsimd", lambda e, i=i, kk=kk:
                              e.indirect_dma_start(
                            out=disp_d[:, :],
                            out_offset=IOA(dsti[:, 16 * kk + i:16 * kk + i + 1], 0),
                            in_=stga(i)[:, :],
                            in_offset=None), dSc[i % 2], 16)
                    if i + 4 < NT:
                        kb.w("sync", dSc[i % 2], dSc[i % 2].n)
                        kb.dma("sync", stga(i + 4),
                               htok_d[128 * (i + 4):128 * (i + 4 + 1), :],
                               stg_sem[sl])
                        stga_dma[sl] = stg_sem[sl].n

            first = state["first_ffn"]
            mm_done = [0, 0]
            eo_ev = [0, 0]
            for e_i in range(E):
                par = e_i % 2
                intc = ab[:, FB_INT + 7168 * par:FB_INT + 7168 * (par + 1)]
                wec = ab[:, FB_WE + 8192 * par:FB_WE + 8192 * (par + 1)]
                kb.w("sync", cT, mm_done[par])
                kb.w("sync", dSc[0], dSc[0].n)
                kb.w("sync", dSc[1], dSc[1].n)
                for dc in range(8):
                    kb.dma("sync", intc[:, 896 * dc:896 * (dc + 1)],
                           disp_d[CPAD * e_i:CPAD * (e_i + 1),
                                  128 * dc:128 * (dc + 1)],
                           dInt[par], transpose=True)
                if first:
                    kb.w("gpsimd", cT, mm_done[par])
                    kb.w("gpsimd", dWb[par], dWb[par].n)
                    for dc in range(8):
                        kb.dma("gpsimd", wec[:, D * dc:D * (dc + 1)],
                               we_d[e_i, 128 * dc:128 * (dc + 1), :], dWeG[par])
                    kb.w("sync", dWeG[par], dWeG[par].n)
                    kb.dma("sync", webf_d[e_i].rearrange("(dc p) f -> p dc f", p=128),
                           wec[:].rearrange("p (dc f) -> p dc f", f=D), dWb[par])
                else:
                    kb.w("gpsimd", cT, mm_done[par])
                    kb.w("gpsimd", dWb[par], dWb[par].n)
                    kb.dma("gpsimd", wec[:].rearrange("p (dc f) -> p dc f", f=D),
                           webf_d[e_i].rearrange("(dc p) f -> p dc f", p=128),
                           dWe[par])

                kb.w("tensor", dInt[par], dInt[par].n)
                kb.w("tensor", dWe[par], dWe[par].n)
                kb.w("tensor", dWeG[par], dWeG[par].n)
                for ct in range(7):
                    mwid = 128 if ct < 6 else 1
                    cp = ct % 2
                    pbank = 2 * cp
                    kb.w("tensor", cA, eo_ev[cp])
                    for f2 in range(2):
                        for dc in range(8):
                            v = kb.op("tensor", lambda e, ct=ct, f2=f2, dc=dc, mwid=mwid,
                                      intc=intc, wec=wec, pbank=pbank: e.matmul(
                                ps[pbank + f2][0:mwid, :],
                                intc[:, 896 * dc + 128 * ct:896 * dc + 128 * ct + mwid],
                                wec[:, D * dc + 512 * f2:D * dc + 512 * (f2 + 1)],
                                start=(dc == 0), stop=(dc == 7)), cT)
                    mm_done[par] = v
                    eop = af[:, FF_EO + D * cp:FF_EO + D * (cp + 1)]
                    kb.w("scalar", cT, mm_done[par])
                    kb.w("scalar", dEo[cp], dEo[cp].n)
                    for f2 in range(2):
                        v = kb.op("scalar", lambda e, f2=f2, eop=eop, pbank=pbank,
                                  mwid=mwid: e.activation(
                            eop[0:mwid, 512 * f2:512 * (f2 + 1)],
                            ps[pbank + f2][0:mwid, :],
                            AF.Copy, bias=0.0, scale=1.0), cA)
                    eo_ev[cp] = v
                    kb.w("gpsimd", cA, eo_ev[cp])
                    r0 = CPAD * e_i + 128 * ct
                    nrow = mwid
                    kb.dma("gpsimd", eout_d[r0:r0 + nrow, :],
                           eop[0:nrow, :], dEo[cp])
            state["first_ffn"] = False

            kb.w("gpsimd", cT, state.get("oproj_cT", 0))
            g_used = [0, 0]
            g_list = []
            for i in range(NT):
                par = i % 2
                g0 = ab[:, FB_G0 + 2048 * par:FB_G0 + 2048 * par + 1024]
                g1 = ab[:, FB_G1 + 2048 * par:FB_G1 + 2048 * par + 1024]
                comb = af[:, FF_COMB + D * par:FF_COMB + D * (par + 1)]
                kb.w("gpsimd", dEo[0], dEo[0].n)
                kb.w("gpsimd", dEo[1], dEo[1].n)
                kb.w("gpsimd", cV, g_used[par])
                for kk, g in ((0, g0), (1, g1)):
                    kb.op("gpsimd", lambda e, kk=kk, g=g, i=i: e.indirect_dma_start(
                        out=g[:, :], out_offset=None,
                        in_=eout_d[:, :],
                        in_offset=IOA(dsti[:, 16 * kk + i:16 * kk + i + 1], 0)),
                        dGa[par], 16)
                kb.w("vector", dGa[par], dGa[par].n)
                kb.w("tensor", dGa[par], dGa[par].n)
                kb.warm(ps[7][0:64, :], wq_sb[:, 0:64], wq_sb[:, 0:512], 3)
                kb.op("vector", lambda e, g0=g0, comb=comb, i=i: e.scalar_tensor_tensor(
                    comb, g0, w0[:, i:i + 1], x_sb[:, D * i:D * (i + 1)],
                    op0=ALU.mult, op1=ALU.add), cV)
                g_used[par] = kb.op("vector", lambda e, g1=g1, comb=comb, i=i:
                                    e.scalar_tensor_tensor(
                    x_sb[:, D * i:D * (i + 1)], g1, w1[:, i:i + 1], comb,
                    op0=ALU.mult, op1=ALU.add), cV)
                if tail_stats:
                    emit_stats(i)
                g_list.append(g_used[par])
            return g_list

        def emit_attn(s, skip_stats=False):
            katt = int(os.environ.get("KATT", "7"))
            emit_ln(s, skip_stats)
            lnT = ab[:, A_LNT:A_LNT + 8 * T]
            qp = ab[:, A_QP:A_QP + 8192]
            kp = ab[:, A_KP:A_KP + 8192]
            kvT = ab[:, A_KVT:A_KVT + T]
            att = ab[:, A_ATT:A_ATT + 8192]
            vsb = ab[:, A_V:A_V + 8448]
            prb = ab[:, A_PR:A_PR + 4096]
            sqs = af[:, AQ_SQS:AQ_SQS + 512]
            sqs2 = af[:, AQ_SQS2:AQ_SQS2 + 512]
            rstd = af[0:4, AQ_RSTD:AQ_RSTD + 512]
            bcast = af[:, AQ_BC:AQ_BC + 512]
            denf = af[:, AQ_DENF:AQ_DENF + 2048]
            recb = af[:, AQ_RECB:AQ_RECB + 512]

            kb.w("sync", dH[0], dH[0].n)
            kb.w("sync", dH[1], dH[1].n)
            kb.w("sync", dWb[0], dWb[0].n)
            kb.w("sync", dWb[1], dWb[1].n)
            tr_hist = [0, 0]
            for dc in range(8):
                kb.w("sync", dTr[dc % 2], tr_hist[dc % 2])
                tr_hist[dc % 2] = dTr[dc % 2].n + 16
                kb.dma("sync", lnT[:, T * dc:T * (dc + 1)],
                       htok_d[:, 128 * dc:128 * (dc + 1)], dTr[dc % 2], transpose=True)
                kb.w("tensor", dTr[dc % 2], dTr[dc % 2].n)
                kb.warm(ps[7][0:64, :], wq_sb[:, 0:64], wq_sb[:, 0:512], 2)
            kb.w("tensor", dTr[0], dTr[0].n)
            kb.w("tensor", dTr[1], dTr[1].n)
            kb.w("tensor", cV, cV.n)
            if katt < 2:
                return
            for c in range(4):
                for dc in range(8):
                    v = kb.op("tensor", lambda e, c=c, dc=dc: e.matmul(
                        ps[6][:, :],
                        wkv_sb[:, LAT * dc:LAT * (dc + 1)],
                        lnT[:, T * dc + 512 * c:T * dc + 512 * (c + 1)],
                        start=(dc == 0), stop=(dc == 7)), cT)
                kb.w("vector", cT, v)
                v_kv = kb.op("vector", lambda e, c=c: e.tensor_copy(
                    kvT[:, 512 * c:512 * (c + 1)], ps[6][:, :]), cV)
                kb.w("tensor", cV, v_kv)

            if katt < 3:
                return
            q_stt = [0, 0]
            q_sqA = [0, 0]
            q_sqB = [0, 0]
            q_ln = 0
            q_bce = 0
            it = 0
            for c in range(4):
                for a in range(4):
                    pp = 2 * (it % 2)
                    it += 1
                    kb.warm(ps[6][0:64, :], wq_sb[:, 0:64], wq_sb[:, 0:512], 2)
                    kb.w("tensor", cV, q_stt[pp // 2])
                    kb.w("tensor", cA, q_sqA[pp // 2])
                    for dc in range(8):
                        v_qa = kb.op("tensor", lambda e, c=c, a=a, dc=dc, pp=pp: e.matmul(
                            ps[pp][:, :],
                            wq_sb[:, D * dc + 128 * a:D * dc + 128 * (a + 1)],
                            lnT[:, T * dc + 512 * c:T * dc + 512 * (c + 1)],
                            start=(dc == 0), stop=(dc == 7)), cT)
                    kb.w("tensor", cA, q_sqB[pp // 2])
                    for dc in range(8):
                        v_qb = kb.op("tensor", lambda e, c=c, a=a, dc=dc, pp=pp: e.matmul(
                            ps[pp + 1][:, :],
                            wq_sb[:, D * dc + 512 + 128 * a:D * dc + 512 + 128 * (a + 1)],
                            lnT[:, T * dc + 512 * c:T * dc + 512 * (c + 1)],
                            start=(dc == 0), stop=(dc == 7)), cT)
                    kb.w("scalar", cT, v_qa)
                    q_sqA[pp // 2] = kb.op("scalar", lambda e, pp=pp: e.activation(
                        sqs[:], ps[pp][:, :], AF.Square, bias=0.0, scale=1.0), cA)
                    kb.w("scalar", cT, v_qb)
                    q_sqB[pp // 2] = kb.op("scalar", lambda e, pp=pp: e.activation(
                        sqs2[:], ps[pp + 1][:, :], AF.Square, bias=0.0, scale=1.0), cA)
                    kb.w("tensor", cA, q_sqB[pp // 2])
                    kb.w("tensor", cA, q_ln)
                    kb.op("tensor", lambda e: e.matmul(
                        ps[4][0:4, :], selk, sqs[:], start=True, stop=False), cT)
                    v_ss = kb.op("tensor", lambda e: e.matmul(
                        ps[4][0:4, :], selk, sqs2[:], start=False, stop=True), cT)
                    kb.w("scalar", cT, v_ss)
                    q_ln = kb.op("scalar", lambda e: e.activation(
                        rstd[0:4, :], ps[4][0:4, :], AF.Ln,
                        bias=1e-6, scale=1.0 / 64.0), cA)
                    v_rs = kb.op("scalar", lambda e: e.activation(
                        rstd[0:4, :], rstd[0:4, :], AF.Exp, bias=-LN8, scale=-0.5), cA)
                    kb.w("tensor", cA, v_rs)
                    kb.w("tensor", cV, q_bce)
                    v_bc = kb.op("tensor", lambda e: e.matmul(
                        ps[5][:, :], bck, rstd[0:4, :], start=True, stop=True), cT)
                    kb.w("vector", cT, v_bc)
                    q_bce = kb.op("vector", lambda e: e.tensor_copy(
                        bcast[:], ps[5][:, :]), cV)
                    q_stt[pp // 2] = kb.op("vector", lambda e, c=c, a=a, pp=pp:
                                           e.scalar_tensor_tensor(
                        qp[:, 2048 * a + 512 * c:2048 * a + 512 * (c + 1)],
                        ps[pp][:, :], qg_col[:, 0:1], bcast[:],
                        op0=ALU.mult, op1=ALU.mult), cV)

            if katt < 4:
                return
            for c in range(4):
                for kc in range(4):
                    pp = 2 * (it % 2)
                    it += 1
                    kb.warm(ps[6][0:64, :], wq_sb[:, 0:64], wq_sb[:, 0:512], 2)
                    kb.w("tensor", cV, q_stt[pp // 2])
                    kb.w("tensor", cA, q_sqA[pp // 2])
                    v_k = kb.op("tensor", lambda e, c=c, kc=kc, pp=pp: e.matmul(
                        ps[pp][:, :],
                        wk_sb[:, 128 * kc:128 * (kc + 1)],
                        kvT[:, 512 * c:512 * (c + 1)],
                        start=True, stop=True), cT)
                    kb.w("scalar", cT, v_k)
                    q_sqA[pp // 2] = kb.op("scalar", lambda e, pp=pp: e.activation(
                        sqs[:], ps[pp][:, :], AF.Square, bias=0.0, scale=1.0), cA)
                    kb.w("tensor", cA, q_sqA[pp // 2])
                    kb.w("tensor", cA, q_ln)
                    v_ss = kb.op("tensor", lambda e: e.matmul(
                        ps[4][0:4, :], selk, sqs[:], start=True, stop=True), cT)
                    kb.w("scalar", cT, v_ss)
                    q_ln = kb.op("scalar", lambda e: e.activation(
                        rstd[0:4, :], ps[4][0:4, :], AF.Ln,
                        bias=1e-6, scale=1.0 / 32.0), cA)
                    v_rs = kb.op("scalar", lambda e: e.activation(
                        rstd[0:4, :], rstd[0:4, :], AF.Exp, bias=0.0, scale=-0.5), cA)
                    kb.w("tensor", cA, v_rs)
                    kb.w("tensor", cV, q_bce)
                    v_bc = kb.op("tensor", lambda e: e.matmul(
                        ps[5][:, :], bck, rstd[0:4, :], start=True, stop=True), cT)
                    kb.w("vector", cT, v_bc)
                    q_bce = kb.op("vector", lambda e: e.tensor_copy(
                        bcast[:], ps[5][:, :]), cV)
                    q_stt[pp // 2] = kb.op("vector", lambda e, kc=kc, c=c, pp=pp:
                                           e.scalar_tensor_tensor(
                        kp[:, 2048 * kc + 512 * c:2048 * kc + 512 * (c + 1)],
                        ps[pp][:, :], kg_col[:, 0:1], bcast[:],
                        op0=ALU.mult, op1=ALU.mult), cV)

            if katt < 5:
                return
            v3d = vsb.rearrange("p (t h d) -> p t h d", h=H, d=33)
            kb.op("vector", lambda e: e.memset(v3d[:, :, :, 32:33], 1.0), cV)
            kb.w("tensor", cA, cA.n)
            kb.w("tensor", cV, cV.n)
            for ti in range(NT):
                par = ti % 2
                v_v = kb.op("tensor", lambda e, ti=ti, par=par: e.matmul(
                    ps[4 + par][:, :], kvT[:, 128 * ti:128 * (ti + 1)],
                    wv_sb[:], start=True, stop=True), cT)
                kb.w("vector", cT, v_v)
                dstv = vsb[:, 528 * ti:528 * (ti + 1)].rearrange(
                    "p (h d) -> p h d", d=33)[:, :, 0:32]
                v_ve = kb.op("vector", lambda e, dstv=dstv, par=par: e.tensor_copy(
                    dstv, ps[4 + par][:, :].rearrange("p (h d) -> p h d", d=32)), cV)
                kb.w("tensor", cV, v_ve)

            if katt < 6:
                return
            v_pb = kb.op("vector", lambda e: e.memset(psB[:, :], 1.0), cV)
            kb.w("tensor", cV, cV.n)
            kb.w("tensor", cA, cA.n)
            exp_done = 0
            ex1_done = 0
            att_tt = 0
            pv_done = [0, 0]
            for hg in range(4):
                for c in range(4):
                    njt = 4 * c + 4
                    def emit_pv(h, j, col0):
                        par = j % 2
                        pr_h = prb[:, 1024 * h + 512 * par + col0:1024 * h + 512 * (par + 1)]
                        vbase = 528 * j + 33 * (4 * hg + h)
                        if h in (0, 2):
                            return kb.op("tensor", lambda e, h=h, vbase=vbase,
                                         pr_h=pr_h, j=j, njt=njt, col0=col0: e.matmul(
                                ps[4 + h][32 * h:32 * h + 33, col0:512],
                                vsb[:, vbase:vbase + 33],
                                pr_h,
                                start=(j == 0), stop=(j == njt - 1),
                                tile_position=(0, 32 * h)), cT)
                        kb.op("tensor", lambda e, h=h, vbase=vbase, pr_h=pr_h,
                              j=j, njt=njt, col0=col0: e.matmul(
                            ps[4 + h][32 * h:32 * h + 32, col0:512],
                            vsb[:, vbase:vbase + 32],
                            pr_h,
                            start=(j == 0), stop=(j == njt - 1),
                            tile_position=(0, 32 * h)), cT)
                        return kb.op("tensor", lambda e, h=h, pr_h=pr_h,
                                     j=j, njt=njt, col0=col0: e.matmul(
                            ps[4 + h][0:1, col0:512],
                            onescol_b, pr_h,
                            start=(j == 0), stop=(j == njt - 1),
                            tile_position=(0, 0),
                            skip_group_check=True), cT)

                    lag = None
                    for j in range(njt):
                        m = j - 4 * c if j >= 4 * c else None
                        col0 = 128 * m if m else 0
                        nwm = 2 if KWARM == 0 else 1
                        wmw = 512 if KWARM == 0 else 128
                        for dwi in range(nwm):
                            kb.op("tensor", lambda e, wmw=wmw: e.matmul(
                                ps[7][32:64, 0:wmw], kp[:, 0:32], kp[:, 0:wmw],
                                start=True, stop=True,
                                skip_group_check=True), None)
                        for h in range(4):
                            kb.w("tensor", cA, ex1_done if h < 2 else exp_done)
                            v_sc = kb.op("tensor", lambda e, hg=hg, c=c, j=j, h=h,
                                         col0=col0: e.matmul(
                                ps[h][:, col0:512],
                                kp[32 * h:32 * (h + 1), 2048 * hg + 128 * j:2048 * hg + 128 * (j + 1)],
                                qp[32 * h:32 * (h + 1), 2048 * hg + 512 * c + col0:2048 * hg + 512 * (c + 1)],
                                start=True, stop=True,
                                tile_position=(32 * h, 0)), cT)
                        if lag is not None:
                            lj, lcol0, lex, lmk = lag
                            kb.w("tensor", cA, lex)
                            kb.w("tensor", cV, lmk)
                            for h in range(4):
                                pvl = emit_pv(h, lj, lcol0)
                            pv_done[lj % 2] = pvl
                        par = j % 2
                        prv = prb.rearrange("p (h q f) -> p h q f", q=2, f=512)[:, :, par, :]
                        psA3 = psA[:, :].rearrange("p (h f) -> p h f", f=512)
                        kb.w("scalar", cT, v_sc)
                        kb.w("scalar", cT, pv_done[par])
                        ex1_done = kb.op("scalar", lambda e, prv=prv, psA3=psA3, col0=col0:
                                      e.activation(
                            prv[:, 0:2, col0:512], psA3[:, 0:2, col0:512],
                            AF.Exp, bias=negc[:, 0:1], scale=1.0), cA)
                        exp_done = kb.op("scalar", lambda e, prv=prv, psA3=psA3, col0=col0:
                                         e.activation(
                            prv[:, 2:4, col0:512], psA3[:, 2:4, col0:512],
                            AF.Exp, bias=negc[:, 0:1], scale=1.0), cA)
                        v_mk2 = 0
                        if m is not None:
                            dmask = cstb[:, 384:512].unsqueeze(1).broadcast_to([128, 4, 128])
                            kb.w("vector", cA, exp_done)
                            v_mk2 = kb.op("vector", lambda e, prv=prv, col0=col0,
                                          dmask=dmask: e.tensor_tensor(
                                prv[:, :, col0:col0 + 128],
                                prv[:, :, col0:col0 + 128], dmask, op=ALU.mult), cV)
                        lag = (j, col0, exp_done, v_mk2)
                    lj, lcol0, lex, lmk = lag
                    kb.w("tensor", cA, lex)
                    kb.w("tensor", cV, lmk)
                    for h in range(4):
                        pvl = emit_pv(h, lj, lcol0)
                        kb.w("vector", cT, pvl)
                        v_dc = kb.op("vector", lambda e, h=h: e.tensor_copy(
                            denf[:, 512 * h:512 * (h + 1)],
                            psB[:, 512 * h:512 * (h + 1)]), cV)
                    pv_done[lj % 2] = pvl

                    DRS = {0: 32, 1: 0, 2: 96, 3: 0}
                    kb.w("tensor", cV, v_dc)
                    kb.w("tensor", cA, exp_done)
                    for h in range(4):
                        dr = DRS[h]
                        v_rb = kb.op("tensor", lambda e, h=h, dr=dr: e.matmul(
                            ps[0][32 * h:32 * (h + 1), :],
                            onesb[dr:dr + 1, :],
                            denf[dr:dr + 1, 512 * h:512 * (h + 1)],
                            start=True, stop=True,
                            tile_position=(dr, 32 * h)), cT)
                    kb.w("scalar", cT, v_rb)
                    kb.w("scalar", cV, att_tt)
                    kb.op("scalar", lambda e: e.activation(
                        recb[:], ps[0][:, :], AF.Ln, bias=0.0, scale=1.0), cA)
                    exp_done = kb.op("scalar", lambda e: e.activation(
                        recb[:], recb[:], AF.Exp, bias=0.0, scale=-1.0), cA)
                    ex1_done = exp_done
                    kb.w("vector", cA, exp_done)
                    for h in range(4):
                        att_tt = kb.op("vector", lambda e, hg=hg, c=c, h=h: e.tensor_tensor(
                            att[32 * h:32 * (h + 1), 2048 * hg + 512 * c:2048 * hg + 512 * (c + 1)],
                            denf[32 * h:32 * h + 32, 512 * h:512 * (h + 1)],
                            recb[32 * h:32 * (h + 1), :],
                            op=ALU.mult), cV)

            state["core_cT"] = cT.n
            state["core_cV"] = cV.n
            if katt < 7:
                return
            kb.w("tensor", cV, cV.n)
            o_ev = [0, 0]
            for ti in range(NT):
                par = ti % 2
                kb.w("tensor", cV, o_ev[par])
                for f2 in range(2):
                    for kc in range(4):
                        v_o = kb.op("tensor", lambda e, ti=ti, f2=f2, kc=kc, par=par: e.matmul(
                            ps[2 * par + f2][:, :],
                            att[:, 2048 * kc + 128 * ti:2048 * kc + 128 * (ti + 1)],
                            wop_sb[:, D * kc + 512 * f2:D * kc + 512 * (f2 + 1)],
                            start=(kc == 0), stop=(kc == 3)), cT)
                kb.w("vector", cT, v_o)
                for f2 in range(2):
                    v = kb.op("vector", lambda e, ti=ti, f2=f2, par=par: e.tensor_tensor(
                        x_sb[:, D * ti + 512 * f2:D * ti + 512 * (f2 + 1)],
                        x_sb[:, D * ti + 512 * f2:D * ti + 512 * (f2 + 1)],
                        ps[2 * par + f2][:, :], op=ALU.add), cV)
                o_ev[par] = v
                emit_stats(ti)
            state["oproj_cT"] = cT.n
            state["oproj_cV"] = cV.n

        lvl = int(os.environ.get("KSTAGES", "5"))
        nseq = SEQ if lvl >= 4 else 1
        for i in range(NT):
            kb.dma("sync", x_sb[:, D * i:D * (i + 1)],
                   x_d[128 * i:128 * (i + 1), :], dX)
        for s in range(nseq):
            g_list = None
            full = lvl >= 4 or (lvl >= 3 and lvl != 20)
            if lvl >= 1 and lvl != 20:
                kb.scope(f"ffn1_s{s}")
                emit_ffn(s, tail_stats=(lvl >= 2))
                kb.escope()
            if lvl >= 2 or lvl == 20:
                kb.scope(f"attn_s{s}")
                emit_attn(s, skip_stats=(lvl != 20))
                kb.escope()
            if lvl >= 3 and lvl != 20:
                kb.scope(f"ffn2_s{s}")
                g_list = emit_ffn(s, skip_stats=True)
                kb.escope()
            last = (s == nseq - 1)
            for i in range(NT):
                if g_list is not None:
                    kb.w("sync", cV, g_list[i])
                else:
                    kb.w("sync", cV, cV.n)
                kb.dma("sync", out_d[T * s + 128 * i:T * s + 128 * (i + 1), :],
                       x_sb[:, D * i:D * (i + 1)], dSt2[i % 2])
                if g_list is not None:
                    kb.w("tensor", cV, g_list[i])
                    kb.warm(ps[7][0:64, :], wq_sb[:, 0:64], wq_sb[:, 0:512], 2)
                if not last and i >= 1:
                    ii = i - 1
                    kb.w("sync", dSt2[ii % 2], dSt2[ii % 2].n)
                    kb.dma("sync", x_sb[:, D * ii:D * (ii + 1)],
                           x_d[T * (s + 1) + 128 * ii:T * (s + 1) + 128 * (ii + 1), :],
                           dX)
            if not last:
                ii = NT - 1
                kb.w("sync", dSt2[ii % 2], dSt2[ii % 2].n)
                kb.dma("sync", x_sb[:, D * ii:D * (ii + 1)],
                       x_d[T * (s + 1) + 128 * ii:T * (s + 1) + 128 * (ii + 1), :],
                       dX)

        kb.em("sync", lambda e, v=dSt2[0].n: e.wait_ge(dSt2[0].h, v))
        kb.em("sync", lambda e, v=dSt2[1].n: e.wait_ge(dSt2[1].h, v))

        with nc.Block() as block:
            block.tensor(lambda eng: [f(eng) for f in kb.th["tensor"]])
            block.vector(lambda eng: [f(eng) for f in kb.th["vector"]])
            block.scalar(lambda eng: [f(eng) for f in kb.th["scalar"]])
            block.gpsimd(lambda eng: [f(eng) for f in kb.th["gpsimd"]])
            block.sync(lambda eng: [f(eng) for f in kb.th["sync"]])

    return nc


def _consts():
    p = np.arange(128)[:, None]
    c128 = np.arange(128)[None, :]
    cf = np.zeros((128, CF), np.float32)
    cf[:, C_IDENT:C_IDENT + 128] = (p == c128)
    cf[:, C_L128:C_L128 + 128] = (p < c128)
    cf[0:16, C_L16:C_L16 + 16] = (np.arange(16)[:, None] < np.arange(16)[None, :])
    cf[:, C_IOTA8:C_IOTA8 + 8] = np.arange(8)[None, :]
    cf[:, C_SELQ:C_SELQ + 2] = (p // 64 == np.arange(2)[None, :])
    cf[:, C_SELK:C_SELK + 4] = (p // 32 == np.arange(4)[None, :])
    cf[0:2, C_BCQ:C_BCQ + 128] = (c128 // 64 == np.arange(2)[:, None])
    cf[0:4, C_BCK:C_BCK + 128] = (c128 // 32 == np.arange(4)[:, None])
    cf[0:1, C_ONESROW:C_ONESROW + 128] = 1.0
    cf[:, C_ONESCOL] = 1.0
    cf[:, C_TRASH] = float(ZROW)
    cf[:, C_ONESB:C_ONESB + 32] = 1.0
    cf[:, C_IMM:C_IMM + len(IMM_VALS)] = np.array(IMM_VALS, np.float32)[None, :]
    cb = np.zeros((128, CB_COLS), np.float32)
    cb[:, 0:896] = (np.arange(896)[None, :] - 384 - p) >= 0
    cb[:, 896] = 1.0
    return cf, cb


def kernel(L0_attn_x, L0_attn_Wq, L0_attn_Wkv, L0_attn_Wk, L0_attn_Wv,
           L0_attn_Wo, L0_attn_q_g, L0_attn_k_g, L0_ffn_Wr, L0_ffn_We):
    x = np.ascontiguousarray(np.asarray(L0_attn_x, np.float32))
    B = x.shape[0]
    wo = np.asarray(L0_attn_Wo, np.float32)
    ridx = np.zeros(H * ROPE, np.int64)
    for g in range(4):
        for j in range(4):
            for dv in range(ROPE):
                ridx[128 * g + 32 * j + dv] = 64 * (4 * g + j) + dv
    wop = np.ascontiguousarray(wo[ridx])
    wrt = np.ascontiguousarray(np.asarray(L0_ffn_Wr, np.float32).T)
    cperm = []
    for half in range(2):
        for h in range(H):
            cperm.extend(range(64 * h + 32 * half, 64 * h + 32 * half + 32))
    wq_p = np.ascontiguousarray(np.asarray(L0_attn_Wq, np.float32)[:, cperm])
    cf, cb = _consts()

    common = dict(
        wq=wq_p,
        wkv=np.ascontiguousarray(np.asarray(L0_attn_Wkv, np.float32)),
        wk=np.ascontiguousarray(np.asarray(L0_attn_Wk, np.float32)),
        wv=np.ascontiguousarray(np.asarray(L0_attn_Wv, np.float32)),
        wop=wop, wrt=wrt,
        we=np.ascontiguousarray(np.asarray(L0_ffn_We, np.float32)),
        qg=np.ascontiguousarray(np.asarray(L0_attn_q_g, np.float32)),
        kg=np.ascontiguousarray(np.asarray(L0_attn_k_g, np.float32)),
        qgc=np.ascontiguousarray(
            np.tile(np.asarray(L0_attn_q_g, np.float32)[:ROPE], 4)[:, None]),
        kgc=np.ascontiguousarray(
            np.tile(np.asarray(L0_attn_k_g, np.float32), 4)[:, None]),
        cstf=cf, cstb=cb,
    )
    in_maps = []
    for core in range(NCORES):
        m = dict(common)
        m["x"] = np.ascontiguousarray(
            x[SEQ * core:SEQ * (core + 1)].reshape(SEQ * T, D))
        in_maps.append(m)

    nc = build_program()
    res = run_bass_kernel_spmd(nc, in_maps, core_ids=list(range(NCORES)))
    out = np.empty((B, T, D), np.float32)
    for core in range(NCORES):
        out[SEQ * core:SEQ * (core + 1)] = (
            res.results[core]["out_x"].reshape(SEQ, T, D))
    return out


# version 49
# speedup vs baseline: 1.0770x; 1.0770x over previous
import math
import os
import contextlib
import numpy as np
import concourse.bass as bass
import concourse.mybir as mybir
from concourse.bass_utils import run_bass_kernel_spmd

F32 = mybir.dt.float32
BF16 = mybir.dt.bfloat16
I32 = mybir.dt.int32
AF = mybir.ActivationFunctionType
ALU = mybir.AluOpType
AX = mybir.AxisListType
IOA = bass.IndirectOffsetOnAxis

NCORES = 8
SELF_WAITS = os.environ.get("KSELFW", "1") == "1"
KWARM = int(os.environ.get("KWARM", "3"))
SEQ = 4
T = 2048
D = 1024
H = 16
HD = 64
ROPE = 32
LAT = 128
E = 8
CAP = 769
CPAD = 896
NSLOT = E * CPAD
ZROW = NSLOT
NT = T // 128
LN8 = math.log(8.0)

C_IDENT = 0
C_L128 = 128
C_L16 = 256
C_IOTA8 = 272
C_SELQ = 280
C_SELK = 282
C_BCQ = 286
C_BCK = 414
C_ONESROW = 542
C_ONESCOL = 670
C_TRASH = 671
C_ONESB = 672
C_IMM = 704
IMM_VALS = [0.0, 1.0, -0.5, 1e-5, 1e-6, -LN8, 1.0 / 64.0, 1.0 / 32.0, -1.0]
CF = 720
CB_COLS = 897

AB_COLS = 43264
A_LNT = 0
A_V = 0
A_PR = 8448
A_QP = 16384
A_KP = 24576
A_KVT = 32768
A_ATT = 34816
FB_WE = 0
FB_INT = 16384
FB_G0 = 30720
FB_G1 = 34816
AB_NEED = max(A_ATT + 8192, FB_G1 + 4096)

AF_COLS = 5376
AQ_SQS = 0
AQ_SQS2 = 512
AQ_RSTD = 1024
AQ_BC = 1536
AQ_DENF = 2048
AQ_RECB = 4096
FF_EO = 0
FF_LGT = 0
FF_COMB = 2048
FF_LTOK = 4096
FF_R128 = 4224
FF_R16 = 4992
FF_SR = 5184
FF_C16 = 5312
FF_CB = 5320
assert FF_CB + 8 <= AF_COLS


class Sem:
    def __init__(self, h):
        self.h = h
        self.n = 0


class KB:

    def __init__(self, nc):
        self.nc = nc
        self.th = {k: [] for k in ("tensor", "vector", "scalar", "gpsimd", "sync")}
        self.sems = []
        self._scope_stack = []
        self.eng_sem = {}

    def sem(self, es, name):
        s = Sem(es.enter_context(self.nc.semaphore(name)))
        self.sems.append(s)
        return s

    def em(self, eng, fn):
        self.th[eng].append(fn)

    def w(self, eng, sem, val):
        if val <= 0:
            return
        self.em(eng, lambda e, s=sem.h, v=val: e.wait_ge(s, v))

    def op(self, eng, fn, sem=None, k=1):
        own = self.eng_sem.get(eng)
        if own is not None and not SELF_WAITS:
            if sem is None:
                sem = own
            own = None
        if own is not None:
            if sem is None:
                sem = own
            self.w(eng, own, own.n)
        if sem is None:
            self.em(eng, fn)
            return None
        sem.n += k
        v = sem.n
        self.em(eng, lambda e, f=fn, s=sem.h, kk=k: f(e).then_inc(s, kk))
        return v

    def dma(self, eng, out, in_, sem, **kw):
        return self.op(
            eng, lambda e, o=out, i=in_, k2=dict(kw): e.dma_start(out=o, in_=i, **k2),
            sem, 16)

    def scope(self, name):
        ids = {}
        for eng in self.th:
            def _enter(e, n=name, eng=eng, ids=ids):
                ids[eng] = self.nc.enter_named_scope(n, False)[0]
            self.em(eng, _enter)
        self._scope_stack.append((name, ids))

    def escope(self):
        name, ids = self._scope_stack.pop()
        for eng in self.th:
            self.em(eng, lambda e, n=name, eng=eng, ids=ids:
                    self.nc.leave_named_scope(n, ids[eng], False))

    def warm(self, tgt, lhsT, rhs, n=2):
        if KWARM >= 3:
            return
        n = max(1, n // (1 + (KWARM > 0)))
        w = 512 // (1 + (KWARM > 1))
        for _ in range(n):
            self.op("tensor", lambda e, t=tgt, l=lhsT, r=rhs, w=w: e.matmul(
                t[:, 0:w], l, r[:, 0:w], start=True, stop=True,
                skip_group_check=True), self.eng_warm)

    def barrier(self):
        snap = [(s, s.n) for s in self.sems if s.n > 0]
        for eng in self.th:
            for s, v in snap:
                self.w(eng, s, v)


def build_program():
    nc = bass.Bass("TRN2", target_bir_lowering=False)

    x_d = nc.dram_tensor("x", [SEQ * T, D], F32, kind="ExternalInput").ap()
    wq_d = nc.dram_tensor("wq", [D, D], F32, kind="ExternalInput").ap()
    wkv_d = nc.dram_tensor("wkv", [D, LAT], F32, kind="ExternalInput").ap()
    wk_d = nc.dram_tensor("wk", [LAT, H * ROPE], F32, kind="ExternalInput").ap()
    wv_d = nc.dram_tensor("wv", [LAT, H * ROPE], F32, kind="ExternalInput").ap()
    wop_d = nc.dram_tensor("wop", [H * ROPE, D], F32, kind="ExternalInput").ap()
    wrt_d = nc.dram_tensor("wrt", [D, E], F32, kind="ExternalInput").ap()
    we_d = nc.dram_tensor("we", [E, D, D], F32, kind="ExternalInput").ap()
    qg_d = nc.dram_tensor("qg", [HD], F32, kind="ExternalInput").ap()
    kg_d = nc.dram_tensor("kg", [ROPE], F32, kind="ExternalInput").ap()
    qgc_d = nc.dram_tensor("qgc", [128, 1], F32, kind="ExternalInput").ap()
    kgc_d = nc.dram_tensor("kgc", [128, 1], F32, kind="ExternalInput").ap()
    cstf_d = nc.dram_tensor("cstf", [128, CF], F32, kind="ExternalInput").ap()
    cstb_d = nc.dram_tensor("cstb", [128, CB_COLS], F32, kind="ExternalInput").ap()

    out_d = nc.dram_tensor("out_x", [SEQ * T, D], F32, kind="ExternalOutput").ap()

    htok_d = nc.dram_tensor("htok", [T, D], BF16, kind="Internal").ap()
    disp_d = nc.dram_tensor("disp", [NSLOT + 128, 1024], BF16, kind="Internal").ap()
    eout_d = nc.dram_tensor("eout", [NSLOT + 128, 1024], BF16, kind="Internal").ap()
    webf_d = nc.dram_tensor("webf", [E, D, D], BF16, kind="Internal").ap()
    csa_d = nc.dram_tensor("csa", [128], F32, kind="Internal").ap()
    csb_d = nc.dram_tensor("csb", [128], F32, kind="Internal").ap()

    es = contextlib.ExitStack()
    with es:
        sb = lambda name, shape, dt: es.enter_context(nc.sbuf_tensor(name, shape, dt))
        x_sb = sb("x_sb", [128, NT * D], F32)
        hst = sb("hst", [128, 2 * D], BF16)
        cstf = sb("cstf_sb", [128, CF], F32)
        cstb = sb("cstb_sb", [128, CB_COLS], BF16)
        wq_sb = sb("wq_sb", [128, 8 * D], BF16)
        wkv_sb = sb("wkv_sb", [128, 8 * LAT], BF16)
        wk_sb = sb("wk_sb", [128, 512], BF16)
        wv_sb = sb("wv_sb", [128, 512], BF16)
        wop_sb = sb("wop_sb", [128, 4 * D], BF16)
        wrt_sb = sb("wrt_sb", [128, 64], BF16)
        qg_col = sb("qg_col", [128, 1], F32)
        kg_col = sb("kg_col", [128, 1], F32)
        qg_row = sb("qg_row", [1, HD], F32)
        kg_row = sb("kg_row", [1, ROPE], F32)
        negc = sb("negc", [128, 1], F32)
        c0t = sb("c0t", [1, 4], F32)
        mvraw = sb("mvraw", [128, 12], F32)
        mv_sb = sb("mv_sb", [128, 2 * NT], F32)
        rstd_ln = sb("rstd_ln", [128, NT], F32)
        bias_ln = sb("bias_ln", [128, NT], F32)
        zero_sb = sb("zero_sb", [128, 512], BF16)
        dsti = sb("dsti", [128, 32], I32)
        ab = sb("arena_bf", [128, AB_NEED], BF16)
        af = sb("arena_f", [128, AF_COLS], F32)
        psA = es.enter_context(nc.psum_tensor("psA", [128, 2048], F32))
        psB = es.enter_context(nc.psum_tensor("psB", [128, 2048], F32))
        ps = [psA[:, 512 * i:512 * (i + 1)] for i in range(4)] + \
             [psB[:, 512 * i:512 * (i + 1)] for i in range(4)]

        kb = KB(nc)
        dSet = kb.sem(es, "dSet")
        dSetG = kb.sem(es, "dSetG")
        dZ = kb.sem(es, "dZ")
        dX = kb.sem(es, "dX")
        dH = [kb.sem(es, "dH0"), kb.sem(es, "dH1")]
        dTr = [kb.sem(es, "dTr0"), kb.sem(es, "dTr1")]
        dInt = [kb.sem(es, "dInt0"), kb.sem(es, "dInt1")]
        dWe = [kb.sem(es, "dWe0"), kb.sem(es, "dWe1")]
        dWeG = [kb.sem(es, "dWeG0"), kb.sem(es, "dWeG1")]
        dWb = [kb.sem(es, "dWb0"), kb.sem(es, "dWb1")]
        dEo = [kb.sem(es, "dEo0"), kb.sem(es, "dEo1")]
        dSc = [kb.sem(es, "dSc0"), kb.sem(es, "dSc1")]
        dGa = [kb.sem(es, "dGa0"), kb.sem(es, "dGa1")]
        dCs = kb.sem(es, "dCs")
        dSg = [kb.sem(es, f"dSg{i}") for i in range(4)]
        dSt2 = [kb.sem(es, "dSt0"), kb.sem(es, "dSt1")]
        cT = kb.sem(es, "cT")
        cV = kb.sem(es, "cV")
        cA = kb.sem(es, "cA")
        cP = kb.sem(es, "cP")
        kb.eng_sem = {"vector": cV, "scalar": cA, "gpsimd": cP}
        kb.eng_warm = cT

        for j, v in enumerate(IMM_VALS):
            nc.const_aps.register(F32, v, cstf[:, C_IMM + j:C_IMM + j + 1])

        ident = cstf[:, C_IDENT:C_IDENT + 128]
        L128 = cstf[:, C_L128:C_L128 + 128]
        L16 = cstf[0:16, C_L16:C_L16 + 16]
        iota8 = cstf[:, C_IOTA8:C_IOTA8 + 8]
        selq = cstf[:, C_SELQ:C_SELQ + 2]
        selk = cstf[:, C_SELK:C_SELK + 4]
        bcq = cstf[0:2, C_BCQ:C_BCQ + 128]
        bck = cstf[0:4, C_BCK:C_BCK + 128]
        onesrow = cstf[0:1, C_ONESROW:C_ONESROW + 128]
        onesb = cstf[:, C_ONESB:C_ONESB + 32]
        ones128 = cstf[:, C_ONESCOL:C_ONESCOL + 1].broadcast_to([128, 128])
        onescol_b = cstb[:, 896:897]
        onescol = cstf[:, C_ONESCOL:C_ONESCOL + 1]
        trash = cstf[:, C_TRASH:C_TRASH + 1]

        v_cst = kb.dma("sync", cstf[:], cstf_d[:, :], dSet)
        kb.dma("gpsimd", cstb[:], cstb_d[:, :], dSetG)
        for dc in range(8):
            kb.dma("gpsimd", wq_sb[:, D * dc:D * (dc + 1)],
                   wq_d[128 * dc:128 * (dc + 1), :], dSetG)
            kb.dma("gpsimd", wkv_sb[:, LAT * dc:LAT * (dc + 1)],
                   wkv_d[128 * dc:128 * (dc + 1), :], dSetG)
        kb.dma("gpsimd", wk_sb[:], wk_d[:, :], dSetG)
        kb.dma("gpsimd", wv_sb[:], wv_d[:, :], dSetG)
        for kc in range(4):
            kb.dma("gpsimd", wop_sb[:, D * kc:D * (kc + 1)],
                   wop_d[128 * kc:128 * (kc + 1), :], dSetG)
        for dc in range(8):
            kb.dma("gpsimd", wrt_sb[:, 8 * dc:8 * (dc + 1)],
                   wrt_d[128 * dc:128 * (dc + 1), :], dSetG)
        kb.dma("sync", qg_col[:], qgc_d[:, :], dSet)
        kb.dma("sync", kg_col[:], kgc_d[:, :], dSet)
        kb.dma("sync", qg_row[:], qg_d.rearrange("(r j) -> r j", r=1), dSet)
        v_kgr = kb.dma("sync", kg_row[:], kg_d.rearrange("(r j) -> r j", r=1), dSet)

        kb.op("vector", lambda e: e.memset(zero_sb[:], 0.0), cV)
        v3d = ab[:, A_V:A_V + 8448].rearrange("p (t h d) -> p t h d", h=H, d=33)

        for b in (disp_d, eout_d):
            kb.w("gpsimd", cV, 1)
            for hf in range(2):
                kb.dma("gpsimd", b[ZROW:ZROW + 128, 512 * hf:512 * (hf + 1)],
                       zero_sb[:], dZ)

        kb.w("vector", dSet, dSet.n)
        kb.w("vector", dSetG, dSetG.n)
        kb.op("vector", lambda e: e.tensor_reduce(
            c0t[:, 0:1], qg_row[:, 0:ROPE], axis=AX.X, op=ALU.max,
            apply_absolute_value=True), cV)
        kb.op("vector", lambda e: e.tensor_reduce(
            c0t[:, 1:2], kg_row[:], axis=AX.X, op=ALU.max,
            apply_absolute_value=True), cV)
        kb.op("vector", lambda e: e.tensor_tensor(
            c0t[:, 2:3], c0t[:, 0:1], c0t[:, 1:2], op=ALU.mult), cV)
        v_c0 = kb.op("vector", lambda e: e.tensor_scalar(
            c0t[:, 3:4], c0t[:, 2:3], -math.sqrt(32.0), None, op0=ALU.mult), cV)
        kb.w("tensor", cV, v_c0)
        kb.w("tensor", dSet, dSet.n)
        v_ncps = kb.op("tensor", lambda e: e.matmul(
            ps[0][0:128, 0:1], onesrow, c0t[:, 3:4], start=True, stop=True), cT)
        kb.w("vector", cT, v_ncps)
        kb.op("vector", lambda e: e.tensor_copy(negc[:], ps[0][0:128, 0:1]), cV)

        kb.barrier()

        state = dict(first_ffn=True)


        def emit_stats(i):
            xt = x_sb[:, D * i:D * (i + 1)]
            kb.op("vector", lambda e, xt=xt: e.bn_stats(
                mvraw[:, 0:6], xt[:, 0:512]), cV)
            kb.op("vector", lambda e, xt=xt: e.bn_stats(
                mvraw[:, 6:12], xt[:, 512:1024]), cV)
            kb.op("vector", lambda e, i=i: e.bn_aggr(
                mv_sb[:, 2 * i:2 * i + 2], mvraw[:]), cV)

        def emit_ln(s, skip_stats=False):
            if not skip_stats:
                kb.w("vector", dX, dX.n)
                for i in range(NT):
                    emit_stats(i)
            v_bn = cV.n
            var_view = mv_sb[:].rearrange("p (t two) -> p two t", two=2)[:, 1, :]
            mean_view = mv_sb[:].rearrange("p (t two) -> p two t", two=2)[:, 0, :]
            kb.w("scalar", cV, v_bn)
            kb.op("scalar", lambda e: e.activation(
                bias_ln[:], var_view, AF.Ln, bias=1e-5, scale=1.0), cA)
            v_l = kb.op("scalar", lambda e: e.activation(
                rstd_ln[:], bias_ln[:], AF.Exp, bias=0.0, scale=-0.5), cA)
            kb.w("vector", cA, v_l)
            v_b = kb.op("vector", lambda e: e.scalar_tensor_tensor(
                bias_ln[:], mean_view, -1.0, rstd_ln[:],
                op0=ALU.mult, op1=ALU.mult), cV)
            kb.w("scalar", cV, v_b)
            kb.w("tensor", cV, cV.n)
            for i in range(NT):
                par = i % 2
                kb.w("scalar", dH[par], dH[par].n)
                v_h = kb.op("scalar", lambda e, i=i, par=par: e.activation(
                    hst[:, D * par:D * (par + 1)], x_sb[:, D * i:D * (i + 1)],
                    AF.Identity, bias=bias_ln[:, i:i + 1],
                    scale=rstd_ln[:, i:i + 1]), cA)
                kb.w("sync", cA, v_h)
                kb.dma("sync", htok_d[128 * i:128 * (i + 1), :],
                       hst[:, D * par:D * (par + 1)], dH[par])
                if i % 4 == 3:
                    kb.w("tensor", cA, v_h)
                    kb.warm(ps[7][0:64, :], wq_sb[:, 0:64], wq_sb[:, 0:512], 4)

        def emit_ffn(s, skip_stats=False, tail_stats=False):
            emit_ln(s, skip_stats)
            lgT = af[:, FF_LGT:FF_LGT + 2048]
            ltok = af[:, FF_LTOK:FF_LTOK + 128]
            R = lambda j: af[:, FF_R128 + 128 * j:FF_R128 + 128 * (j + 1)]
            oh0, oh1, ohsum, prefix, pos_all, tmp = (R(j) for j in range(6))
            S = lambda j: af[:, FF_R16 + 16 * j:FF_R16 + 16 * (j + 1)]
            m0, m1, idx0, idx1, pos0, pos1, w0, w1, drop, d0f, d1f, _sp = (
                S(j) for j in range(12))
            sr = af[0:1, FF_SR:FF_SR + 128]
            c16 = af[0:16, FF_C16:FF_C16 + 8]
            cb16 = af[0:16, FF_CB:FF_CB + 8]

            kb.w("tensor", cV, state.get("oproj_cV", 0))
            lnTc = ab[:, A_LNT:A_LNT + 4 * T]
            kb.w("sync", cT, state.get("core_cT", 0))
            kb.w("sync", cV, state.get("core_cV", 0))
            v_mm_prev = [0, 0, 0, 0]
            tr_hist = [0, 0]
            for dc in range(8):
                par = dc % 4
                kb.w("sync", dH[0], dH[0].n)
                kb.w("sync", dH[1], dH[1].n)
                kb.w("sync", cT, v_mm_prev[par])
                kb.w("sync", dTr[dc % 2], tr_hist[dc % 2])
                tr_hist[dc % 2] = dTr[dc % 2].n + 16
                kb.dma("sync", lnTc[:, T * par:T * (par + 1)],
                       htok_d[:, 128 * dc:128 * (dc + 1)], dTr[dc % 2], transpose=True)
                kb.w("tensor", dTr[dc % 2], dTr[dc % 2].n)
                kb.warm(ps[7][0:64, :], wq_sb[:, 0:64], wq_sb[:, 0:512], 1)
                for c in range(4):
                    v = kb.op("tensor", lambda e, dc=dc, c=c, par=par: e.matmul(
                        ps[c][0:8, :],
                        wrt_sb[:, 8 * dc:8 * (dc + 1)],
                        lnTc[:, T * par + 512 * c:T * par + 512 * (c + 1)],
                        start=(dc == 0), stop=(dc == 7)), cT)
                v_mm_prev[par] = v
            v_lgmm = cT.n
            kb.w("vector", cT, v_lgmm)
            for c in range(4):
                kb.op("vector", lambda e, c=c: e.tensor_copy(
                    lgT[0:8, 512 * c:512 * (c + 1)], ps[c][0:8, :]), cV)
            v_lgev = cV.n

            stg = ab[:, 16384:16384 + 4096]

            def stga(i):
                return stg[:, D * (i % 4):D * (i % 4 + 1)]

            kb.w("sync", dH[0], dH[0].n)
            kb.w("sync", dH[1], dH[1].n)
            stga_dma = [0, 0, 0, 0]
            stg_sem = dSg
            for i in range(4):
                kb.dma("sync", stga(i), htok_d[128 * i:128 * (i + 1), :],
                       stg_sem[i])
                stga_dma[i] = stg_sem[i].n

            l3 = ltok.rearrange("p (t e) -> p t e", e=8)
            o3 = lambda a_: a_.rearrange("p (t e) -> p t e", e=8)
            i3h = iota8.unsqueeze(1).broadcast_to([128, 8, 8])
            scat_hist = [0, 0]
            kb.w("tensor", cV, v_lgev)
            for half in range(2):
                hs = slice(8 * half, 8 * half + 8)
                cs = slice(64 * half, 64 * half + 64)
                for i in range(8 * half, 8 * half + 8):
                    kb.op("tensor", lambda e, i=i: e.transpose(
                        ps[4][:, 8 * i:8 * (i + 1)],
                        lgT[0:8, 128 * i:128 * (i + 1)], ident[0:8, 0:8]), cT)
                v_lt = cT.n
                kb.w("vector", cT, v_lt)
                kb.op("vector", lambda e, cs=cs: e.tensor_copy(
                    ltok[:, cs], ps[4][:, cs]), cV)
                l3h = l3[:, hs, :]
                mb = lambda a_: a_[:, hs].unsqueeze(2).broadcast_to([128, 8, 8])
                kb.op("vector", lambda e, l3h=l3h, hs=hs: e.tensor_reduce(
                    m0[:, hs], l3h, axis=AX.X, op=ALU.max), cV)
                kb.op("vector", lambda e, l3h=l3h, hs=hs, mbv=mb(m0): e.tensor_tensor(
                    o3(oh0)[:, hs, :], l3h, mbv, op=ALU.is_ge), cV)
                kb.op("vector", lambda e, hs=hs, cs=cs: e.scalar_tensor_tensor(
                    tmp[:, cs], oh0[:, cs], -1e30, ltok[:, cs],
                    op0=ALU.mult, op1=ALU.add), cV)
                kb.op("vector", lambda e, hs=hs, cs=cs: e.tensor_reduce(
                    m1[:, hs], o3(tmp)[:, hs, :], axis=AX.X, op=ALU.max), cV)
                kb.op("vector", lambda e, hs=hs, mbv=mb(m1), cs=cs: e.tensor_tensor(
                    o3(oh1)[:, hs, :], o3(tmp)[:, hs, :], mbv, op=ALU.is_ge), cV)
                kb.op("vector", lambda e, hs=hs: e.tensor_tensor(
                    o3(tmp)[:, hs, :], o3(oh0)[:, hs, :], i3h, op=ALU.mult), cV)
                kb.op("vector", lambda e, hs=hs: e.tensor_reduce(
                    idx0[:, hs], o3(tmp)[:, hs, :], axis=AX.X, op=ALU.add), cV)
                kb.op("vector", lambda e, hs=hs: e.tensor_tensor(
                    o3(tmp)[:, hs, :], o3(oh1)[:, hs, :], i3h, op=ALU.mult), cV)
                kb.op("vector", lambda e, hs=hs: e.tensor_reduce(
                    idx1[:, hs], o3(tmp)[:, hs, :], axis=AX.X, op=ALU.add), cV)
                v_m = kb.op("vector", lambda e, hs=hs: e.tensor_tensor(
                    w1[:, hs], m1[:, hs], m0[:, hs], op=ALU.subtract), cV)
                kb.w("scalar", cV, v_m)
                v_e1 = kb.op("scalar", lambda e, hs=hs: e.activation(
                    w1[:, hs], w1[:, hs], AF.Exp, bias=0.0, scale=1.0), cA)
                kb.w("vector", cA, v_e1)
                kb.op("vector", lambda e, hs=hs: e.tensor_scalar(
                    drop[:, hs], w1[:, hs], 1.0, None, op0=ALU.add), cV)
                kb.op("vector", lambda e, hs=hs: e.reciprocal(
                    w0[:, hs], drop[:, hs]), cV)
                kb.op("vector", lambda e, hs=hs: e.tensor_tensor(
                    w1[:, hs], w1[:, hs], w0[:, hs], op=ALU.mult), cV)
                kb.op("vector", lambda e, hs=hs: e.tensor_scalar(
                    w1[:, hs], w1[:, hs], 0.5, None, op0=ALU.mult), cV)
                kb.op("vector", lambda e, hs=hs: e.tensor_scalar(
                    w0[:, hs], w0[:, hs], 0.5, None, op0=ALU.mult), cV)
                v_ohs = kb.op("vector", lambda e, hs=hs, cs=cs: e.tensor_tensor(
                    ohsum[:, cs], oh0[:, cs], oh1[:, cs], op=ALU.add), cV)
                kb.w("tensor", cV, v_ohs)
                pbank = ps[5 + half]
                for cch in range(8 * half, 8 * half + 8):
                    kb.op("tensor", lambda e, cch=cch, pbank=pbank, half=half: e.matmul(
                        pbank[:, 8 * (cch - 8 * half):8 * (cch - 8 * half) + 8], L128,
                        ohsum[:, 8 * cch:8 * (cch + 1)],
                        start=(cch == 8 * half), stop=False,
                        skip_group_check=True), cT)
                v_csum = cT.n
                lo = 8 * half
                for cch in range(0, 8 * half + 7):
                    tgt0 = max(cch + 1, lo)
                    rep = 8 * half + 8 - tgt0
                    if rep <= 0:
                        continue
                    rhs_rep = ohsum[:, 8 * cch:8 * (cch + 1)].unsqueeze(1) \
                        .broadcast_to([128, rep, 8])
                    v_csum = kb.op("tensor", lambda e, rhs_rep=rhs_rep, rep=rep,
                                   tgt0=tgt0, pbank=pbank, lo=lo: e.matmul(
                        pbank[:, 8 * (tgt0 - lo):8 * (tgt0 - lo) + 8 * rep],
                        ones128, rhs_rep,
                        start=False, stop=(tgt0 == lo and rep == 8 and False),
                        skip_group_check=True), cT)
                kb.w("vector", cT, v_csum)
                kb.op("vector", lambda e, cs=cs, pbank=pbank: e.tensor_copy(
                    pos_all[:, cs], pbank[:, 0:64]), cV)
                for posx, idxx, dfx, col, ohx in (
                        (pos0, idx0, d0f, 0, oh0), (pos1, idx1, d1f, 16, oh1)):
                    kb.op("vector", lambda e, hs=hs, cs=cs, ohx=ohx: e.tensor_tensor(
                        o3(tmp)[:, hs, :], o3(ohx)[:, hs, :], o3(pos_all)[:, hs, :],
                        op=ALU.mult), cV)
                    kb.op("vector", lambda e, p=posx, hs=hs: e.tensor_reduce(
                        p[:, hs], o3(tmp)[:, hs, :], axis=AX.X, op=ALU.add), cV)
                    kb.op("vector", lambda e, p=posx, hs=hs: e.tensor_scalar(
                        drop[:, hs], p[:, hs], float(CAP) - 0.5, None,
                        op0=ALU.is_gt), cV)
                    kb.op("vector", lambda e, p=posx, ix=idxx, d=dfx, hs=hs:
                          e.scalar_tensor_tensor(
                        d[:, hs], ix[:, hs], float(CPAD), p[:, hs],
                        op0=ALU.mult, op1=ALU.add), cV)
                    kb.op("vector", lambda e, d=dfx, hs=hs: e.tensor_scalar(
                        _sp[:, hs], d[:, hs], -1.0, float(ZROW),
                        op0=ALU.mult, op1=ALU.add), cV)
                    kb.op("vector", lambda e, hs=hs: e.tensor_tensor(
                        _sp[:, hs], _sp[:, hs], drop[:, hs], op=ALU.mult), cV)
                    kb.op("vector", lambda e, d=dfx, hs=hs: e.tensor_tensor(
                        _sp[:, hs], _sp[:, hs], d[:, hs], op=ALU.add), cV)
                    v_route = kb.op("vector", lambda e, col=col, hs=hs, half=half:
                                    e.tensor_copy(
                        dsti[:, col + 8 * half:col + 8 * half + 8], _sp[:, hs]), cV)
                kb.w("gpsimd", cV, v_route)
                for i in range(8 * half, 8 * half + 8):
                    sl = i % 4
                    kb.w("gpsimd", stg_sem[sl], stga_dma[sl])
                    kb.w("gpsimd", dSc[i % 2], scat_hist[i % 2])
                    scat_hist[i % 2] = dSc[i % 2].n + 32
                    kb.w("tensor", dSc[i % 2], dSc[i % 2].n)
                    kb.warm(ps[7][0:64, :], wq_sb[:, 0:64], wq_sb[:, 0:512], 3)
                    for kk in range(2):
                        kb.op("gpsimd", lambda e, i=i, kk=kk:
                              e.indirect_dma_start(
                            out=disp_d[:, :],
                            out_offset=IOA(dsti[:, 16 * kk + i:16 * kk + i + 1], 0),
                            in_=stga(i)[:, :],
                            in_offset=None), dSc[i % 2], 16)
                    if i + 4 < NT:
                        kb.w("sync", dSc[i % 2], dSc[i % 2].n)
                        kb.dma("sync", stga(i + 4),
                               htok_d[128 * (i + 4):128 * (i + 4 + 1), :],
                               stg_sem[sl])
                        stga_dma[sl] = stg_sem[sl].n

            first = state["first_ffn"]
            mm_done = [0, 0]
            eo_ev = [0, 0]
            for e_i in range(E):
                par = e_i % 2
                intc = ab[:, FB_INT + 7168 * par:FB_INT + 7168 * (par + 1)]
                wec = ab[:, FB_WE + 8192 * par:FB_WE + 8192 * (par + 1)]
                kb.w("sync", cT, mm_done[par])
                kb.w("sync", dSc[0], dSc[0].n)
                kb.w("sync", dSc[1], dSc[1].n)
                for dc in range(8):
                    kb.dma("sync", intc[:, 896 * dc:896 * (dc + 1)],
                           disp_d[CPAD * e_i:CPAD * (e_i + 1),
                                  128 * dc:128 * (dc + 1)],
                           dInt[par], transpose=True)
                if first:
                    kb.w("gpsimd", cT, mm_done[par])
                    kb.w("gpsimd", dWb[par], dWb[par].n)
                    for dc in range(8):
                        kb.dma("gpsimd", wec[:, D * dc:D * (dc + 1)],
                               we_d[e_i, 128 * dc:128 * (dc + 1), :], dWeG[par])
                    kb.w("sync", dWeG[par], dWeG[par].n)
                    kb.dma("sync", webf_d[e_i].rearrange("(dc p) f -> p dc f", p=128),
                           wec[:].rearrange("p (dc f) -> p dc f", f=D), dWb[par])
                else:
                    kb.dma("sync", wec[:].rearrange("p (dc f) -> p dc f", f=D),
                           webf_d[e_i].rearrange("(dc p) f -> p dc f", p=128),
                           dWe[par])

                for _ in range(2):
                    kb.op("tensor", lambda e: e.matmul(
                        ps[7][0:64, :], wq_sb[:, 0:64], wq_sb[:, 0:512],
                        start=True, stop=True, skip_group_check=True), None)
                kb.w("tensor", dInt[par], dInt[par].n)
                kb.w("tensor", dWe[par], dWe[par].n)
                kb.w("tensor", dWeG[par], dWeG[par].n)
                for ct in range(7):
                    mwid = 128 if ct < 6 else 1
                    cp = ct % 2
                    pbank = 2 * cp
                    kb.w("tensor", cA, eo_ev[cp])
                    for f2 in range(2):
                        for dc in range(8):
                            v = kb.op("tensor", lambda e, ct=ct, f2=f2, dc=dc, mwid=mwid,
                                      intc=intc, wec=wec, pbank=pbank: e.matmul(
                                ps[pbank + f2][0:mwid, :],
                                intc[:, 896 * dc + 128 * ct:896 * dc + 128 * ct + mwid],
                                wec[:, D * dc + 512 * f2:D * dc + 512 * (f2 + 1)],
                                start=(dc == 0), stop=(dc == 7)), cT)
                    mm_done[par] = v
                    eop = af[:, FF_EO + D * cp:FF_EO + D * (cp + 1)]
                    kb.w("scalar", cT, mm_done[par])
                    kb.w("scalar", dEo[cp], dEo[cp].n)
                    for f2 in range(2):
                        v = kb.op("scalar", lambda e, f2=f2, eop=eop, pbank=pbank,
                                  mwid=mwid: e.activation(
                            eop[0:mwid, 512 * f2:512 * (f2 + 1)],
                            ps[pbank + f2][0:mwid, :],
                            AF.Copy, bias=0.0, scale=1.0), cA)
                    eo_ev[cp] = v
                    kb.w("gpsimd", cA, eo_ev[cp])
                    r0 = CPAD * e_i + 128 * ct
                    nrow = mwid
                    kb.dma("gpsimd", eout_d[r0:r0 + nrow, :],
                           eop[0:nrow, :], dEo[cp])
            state["first_ffn"] = False

            kb.w("gpsimd", cT, state.get("oproj_cT", 0))
            g_used = [0, 0]
            g_list = []
            for i in range(NT):
                par = i % 2
                g0 = ab[:, FB_G0 + 2048 * par:FB_G0 + 2048 * par + 1024]
                g1 = ab[:, FB_G1 + 2048 * par:FB_G1 + 2048 * par + 1024]
                comb = af[:, FF_COMB + D * par:FF_COMB + D * (par + 1)]
                kb.w("gpsimd", dEo[0], dEo[0].n)
                kb.w("gpsimd", dEo[1], dEo[1].n)
                kb.w("gpsimd", cV, g_used[par])
                for kk, g in ((0, g0), (1, g1)):
                    kb.op("gpsimd", lambda e, kk=kk, g=g, i=i: e.indirect_dma_start(
                        out=g[:, :], out_offset=None,
                        in_=eout_d[:, :],
                        in_offset=IOA(dsti[:, 16 * kk + i:16 * kk + i + 1], 0)),
                        dGa[par], 16)
                kb.w("vector", dGa[par], dGa[par].n)
                kb.w("tensor", dGa[par], dGa[par].n)
                kb.warm(ps[7][0:64, :], wq_sb[:, 0:64], wq_sb[:, 0:512], 3)
                kb.op("vector", lambda e, g0=g0, comb=comb, i=i: e.scalar_tensor_tensor(
                    comb, g0, w0[:, i:i + 1], x_sb[:, D * i:D * (i + 1)],
                    op0=ALU.mult, op1=ALU.add), cV)
                g_used[par] = kb.op("vector", lambda e, g1=g1, comb=comb, i=i:
                                    e.scalar_tensor_tensor(
                    x_sb[:, D * i:D * (i + 1)], g1, w1[:, i:i + 1], comb,
                    op0=ALU.mult, op1=ALU.add), cV)
                if tail_stats:
                    emit_stats(i)
                g_list.append(g_used[par])
            return g_list

        def emit_attn(s, skip_stats=False):
            katt = int(os.environ.get("KATT", "7"))
            emit_ln(s, skip_stats)
            lnT = ab[:, A_LNT:A_LNT + 8 * T]
            qp = ab[:, A_QP:A_QP + 8192]
            kp = ab[:, A_KP:A_KP + 8192]
            kvT = ab[:, A_KVT:A_KVT + T]
            att = ab[:, A_ATT:A_ATT + 8192]
            vsb = ab[:, A_V:A_V + 8448]
            prb = ab[:, A_PR:A_PR + 4096]
            sqs = af[:, AQ_SQS:AQ_SQS + 512]
            sqs2 = af[:, AQ_SQS2:AQ_SQS2 + 512]
            rstd = af[0:4, AQ_RSTD:AQ_RSTD + 512]
            bcast = af[:, AQ_BC:AQ_BC + 512]
            denf = af[:, AQ_DENF:AQ_DENF + 2048]
            recb = af[:, AQ_RECB:AQ_RECB + 512]

            kb.w("sync", dH[0], dH[0].n)
            kb.w("sync", dH[1], dH[1].n)
            kb.w("sync", dWb[0], dWb[0].n)
            kb.w("sync", dWb[1], dWb[1].n)
            tr_hist = [0, 0]
            for dc in range(8):
                kb.w("sync", dTr[dc % 2], tr_hist[dc % 2])
                tr_hist[dc % 2] = dTr[dc % 2].n + 16
                kb.dma("sync", lnT[:, T * dc:T * (dc + 1)],
                       htok_d[:, 128 * dc:128 * (dc + 1)], dTr[dc % 2], transpose=True)
                kb.w("tensor", dTr[dc % 2], dTr[dc % 2].n)
                kb.warm(ps[7][0:64, :], wq_sb[:, 0:64], wq_sb[:, 0:512], 2)
            kb.w("tensor", dTr[0], dTr[0].n)
            kb.w("tensor", dTr[1], dTr[1].n)
            kb.w("tensor", cV, cV.n)
            if katt < 2:
                return
            for c in range(4):
                for dc in range(8):
                    v = kb.op("tensor", lambda e, c=c, dc=dc: e.matmul(
                        ps[6][:, :],
                        wkv_sb[:, LAT * dc:LAT * (dc + 1)],
                        lnT[:, T * dc + 512 * c:T * dc + 512 * (c + 1)],
                        start=(dc == 0), stop=(dc == 7)), cT)
                kb.w("vector", cT, v)
                v_kv = kb.op("vector", lambda e, c=c: e.tensor_copy(
                    kvT[:, 512 * c:512 * (c + 1)], ps[6][:, :]), cV)
                kb.w("tensor", cV, v_kv)

            if katt < 3:
                return
            q_stt = [0, 0]
            q_sqA = [0, 0]
            q_sqB = [0, 0]
            q_ln = 0
            q_bce = 0
            it = 0
            for c in range(4):
                for a in range(4):
                    pp = 2 * (it % 2)
                    it += 1
                    kb.warm(ps[6][0:64, :], wq_sb[:, 0:64], wq_sb[:, 0:512], 2)
                    kb.w("tensor", cV, q_stt[pp // 2])
                    kb.w("tensor", cA, q_sqA[pp // 2])
                    for dc in range(8):
                        v_qa = kb.op("tensor", lambda e, c=c, a=a, dc=dc, pp=pp: e.matmul(
                            ps[pp][:, :],
                            wq_sb[:, D * dc + 128 * a:D * dc + 128 * (a + 1)],
                            lnT[:, T * dc + 512 * c:T * dc + 512 * (c + 1)],
                            start=(dc == 0), stop=(dc == 7)), cT)
                    kb.w("tensor", cA, q_sqB[pp // 2])
                    for dc in range(8):
                        v_qb = kb.op("tensor", lambda e, c=c, a=a, dc=dc, pp=pp: e.matmul(
                            ps[pp + 1][:, :],
                            wq_sb[:, D * dc + 512 + 128 * a:D * dc + 512 + 128 * (a + 1)],
                            lnT[:, T * dc + 512 * c:T * dc + 512 * (c + 1)],
                            start=(dc == 0), stop=(dc == 7)), cT)
                    kb.w("scalar", cT, v_qa)
                    q_sqA[pp // 2] = kb.op("scalar", lambda e, pp=pp: e.activation(
                        sqs[:], ps[pp][:, :], AF.Square, bias=0.0, scale=1.0), cA)
                    kb.w("scalar", cT, v_qb)
                    q_sqB[pp // 2] = kb.op("scalar", lambda e, pp=pp: e.activation(
                        sqs2[:], ps[pp + 1][:, :], AF.Square, bias=0.0, scale=1.0), cA)
                    kb.w("tensor", cA, q_sqB[pp // 2])
                    kb.w("tensor", cA, q_ln)
                    kb.op("tensor", lambda e: e.matmul(
                        ps[4][0:4, :], selk, sqs[:], start=True, stop=False), cT)
                    v_ss = kb.op("tensor", lambda e: e.matmul(
                        ps[4][0:4, :], selk, sqs2[:], start=False, stop=True), cT)
                    kb.w("scalar", cT, v_ss)
                    q_ln = kb.op("scalar", lambda e: e.activation(
                        rstd[0:4, :], ps[4][0:4, :], AF.Ln,
                        bias=1e-6, scale=1.0 / 64.0), cA)
                    v_rs = kb.op("scalar", lambda e: e.activation(
                        rstd[0:4, :], rstd[0:4, :], AF.Exp, bias=-LN8, scale=-0.5), cA)
                    kb.w("tensor", cA, v_rs)
                    kb.w("tensor", cV, q_bce)
                    v_bc = kb.op("tensor", lambda e: e.matmul(
                        ps[5][:, :], bck, rstd[0:4, :], start=True, stop=True), cT)
                    kb.w("vector", cT, v_bc)
                    q_bce = kb.op("vector", lambda e: e.tensor_copy(
                        bcast[:], ps[5][:, :]), cV)
                    q_stt[pp // 2] = kb.op("vector", lambda e, c=c, a=a, pp=pp:
                                           e.scalar_tensor_tensor(
                        qp[:, 2048 * a + 512 * c:2048 * a + 512 * (c + 1)],
                        ps[pp][:, :], qg_col[:, 0:1], bcast[:],
                        op0=ALU.mult, op1=ALU.mult), cV)

            if katt < 4:
                return
            for c in range(4):
                for kc in range(4):
                    pp = 2 * (it % 2)
                    it += 1
                    kb.warm(ps[6][0:64, :], wq_sb[:, 0:64], wq_sb[:, 0:512], 2)
                    kb.w("tensor", cV, q_stt[pp // 2])
                    kb.w("tensor", cA, q_sqA[pp // 2])
                    v_k = kb.op("tensor", lambda e, c=c, kc=kc, pp=pp: e.matmul(
                        ps[pp][:, :],
                        wk_sb[:, 128 * kc:128 * (kc + 1)],
                        kvT[:, 512 * c:512 * (c + 1)],
                        start=True, stop=True), cT)
                    kb.w("scalar", cT, v_k)
                    q_sqA[pp // 2] = kb.op("scalar", lambda e, pp=pp: e.activation(
                        sqs[:], ps[pp][:, :], AF.Square, bias=0.0, scale=1.0), cA)
                    kb.w("tensor", cA, q_sqA[pp // 2])
                    kb.w("tensor", cA, q_ln)
                    v_ss = kb.op("tensor", lambda e: e.matmul(
                        ps[4][0:4, :], selk, sqs[:], start=True, stop=True), cT)
                    kb.w("scalar", cT, v_ss)
                    q_ln = kb.op("scalar", lambda e: e.activation(
                        rstd[0:4, :], ps[4][0:4, :], AF.Ln,
                        bias=1e-6, scale=1.0 / 32.0), cA)
                    v_rs = kb.op("scalar", lambda e: e.activation(
                        rstd[0:4, :], rstd[0:4, :], AF.Exp, bias=0.0, scale=-0.5), cA)
                    kb.w("tensor", cA, v_rs)
                    kb.w("tensor", cV, q_bce)
                    v_bc = kb.op("tensor", lambda e: e.matmul(
                        ps[5][:, :], bck, rstd[0:4, :], start=True, stop=True), cT)
                    kb.w("vector", cT, v_bc)
                    q_bce = kb.op("vector", lambda e: e.tensor_copy(
                        bcast[:], ps[5][:, :]), cV)
                    q_stt[pp // 2] = kb.op("vector", lambda e, kc=kc, c=c, pp=pp:
                                           e.scalar_tensor_tensor(
                        kp[:, 2048 * kc + 512 * c:2048 * kc + 512 * (c + 1)],
                        ps[pp][:, :], kg_col[:, 0:1], bcast[:],
                        op0=ALU.mult, op1=ALU.mult), cV)

            if katt < 5:
                return
            v3d = vsb.rearrange("p (t h d) -> p t h d", h=H, d=33)
            kb.op("vector", lambda e: e.memset(v3d[:, :, :, 32:33], 1.0), cV)
            kb.w("tensor", cA, cA.n)
            kb.w("tensor", cV, cV.n)
            for ti in range(NT):
                par = ti % 2
                v_v = kb.op("tensor", lambda e, ti=ti, par=par: e.matmul(
                    ps[4 + par][:, :], kvT[:, 128 * ti:128 * (ti + 1)],
                    wv_sb[:], start=True, stop=True), cT)
                kb.w("vector", cT, v_v)
                dstv = vsb[:, 528 * ti:528 * (ti + 1)].rearrange(
                    "p (h d) -> p h d", d=33)[:, :, 0:32]
                v_ve = kb.op("vector", lambda e, dstv=dstv, par=par: e.tensor_copy(
                    dstv, ps[4 + par][:, :].rearrange("p (h d) -> p h d", d=32)), cV)
                kb.w("tensor", cV, v_ve)

            if katt < 6:
                return
            v_pb = kb.op("vector", lambda e: e.memset(psB[:, :], 1.0), cV)
            kb.w("tensor", cV, cV.n)
            kb.w("tensor", cA, cA.n)
            exp_done = 0
            ex1_done = 0
            att_tt = 0
            pv_done = [0, 0]
            for hg in range(4):
                for c in range(4):
                    njt = 4 * c + 4
                    def emit_pv(h, j, col0):
                        par = j % 2
                        pr_h = prb[:, 1024 * h + 512 * par + col0:1024 * h + 512 * (par + 1)]
                        vbase = 528 * j + 33 * (4 * hg + h)
                        if h in (0, 2):
                            return kb.op("tensor", lambda e, h=h, vbase=vbase,
                                         pr_h=pr_h, j=j, njt=njt, col0=col0: e.matmul(
                                ps[4 + h][32 * h:32 * h + 33, col0:512],
                                vsb[:, vbase:vbase + 33],
                                pr_h,
                                start=(j == 0), stop=(j == njt - 1),
                                tile_position=(0, 32 * h)), cT)
                        kb.op("tensor", lambda e, h=h, vbase=vbase, pr_h=pr_h,
                              j=j, njt=njt, col0=col0: e.matmul(
                            ps[4 + h][32 * h:32 * h + 32, col0:512],
                            vsb[:, vbase:vbase + 32],
                            pr_h,
                            start=(j == 0), stop=(j == njt - 1),
                            tile_position=(0, 32 * h)), cT)
                        return kb.op("tensor", lambda e, h=h, pr_h=pr_h,
                                     j=j, njt=njt, col0=col0: e.matmul(
                            ps[4 + h][0:1, col0:512],
                            onescol_b, pr_h,
                            start=(j == 0), stop=(j == njt - 1),
                            tile_position=(0, 0),
                            skip_group_check=True), cT)

                    lag = None
                    for j in range(njt):
                        m = j - 4 * c if j >= 4 * c else None
                        col0 = 128 * m if m else 0
                        nwm = 2 if KWARM == 0 else 1
                        wmw = 512 if KWARM == 0 else 128
                        for dwi in range(nwm):
                            kb.op("tensor", lambda e, wmw=wmw: e.matmul(
                                ps[7][32:64, 0:wmw], kp[:, 0:32], kp[:, 0:wmw],
                                start=True, stop=True,
                                skip_group_check=True), None)
                        for h in range(4):
                            kb.w("tensor", cA, ex1_done if h < 2 else exp_done)
                            v_sc = kb.op("tensor", lambda e, hg=hg, c=c, j=j, h=h,
                                         col0=col0: e.matmul(
                                ps[h][:, col0:512],
                                kp[32 * h:32 * (h + 1), 2048 * hg + 128 * j:2048 * hg + 128 * (j + 1)],
                                qp[32 * h:32 * (h + 1), 2048 * hg + 512 * c + col0:2048 * hg + 512 * (c + 1)],
                                start=True, stop=True,
                                tile_position=(32 * h, 0)), cT)
                        if lag is not None:
                            lj, lcol0, lex, lmk = lag
                            kb.w("tensor", cA, lex)
                            kb.w("tensor", cV, lmk)
                            for h in range(4):
                                pvl = emit_pv(h, lj, lcol0)
                            pv_done[lj % 2] = pvl
                        par = j % 2
                        prv = prb.rearrange("p (h q f) -> p h q f", q=2, f=512)[:, :, par, :]
                        psA3 = psA[:, :].rearrange("p (h f) -> p h f", f=512)
                        kb.w("scalar", cT, v_sc)
                        kb.w("scalar", cT, pv_done[par])
                        ex1_done = kb.op("scalar", lambda e, prv=prv, psA3=psA3, col0=col0:
                                      e.activation(
                            prv[:, 0:2, col0:512], psA3[:, 0:2, col0:512],
                            AF.Exp, bias=negc[:, 0:1], scale=1.0), cA)
                        exp_done = kb.op("scalar", lambda e, prv=prv, psA3=psA3, col0=col0:
                                         e.activation(
                            prv[:, 2:4, col0:512], psA3[:, 2:4, col0:512],
                            AF.Exp, bias=negc[:, 0:1], scale=1.0), cA)
                        v_mk2 = 0
                        if m is not None:
                            dmask = cstb[:, 384:512].unsqueeze(1).broadcast_to([128, 4, 128])
                            kb.w("vector", cA, exp_done)
                            v_mk2 = kb.op("vector", lambda e, prv=prv, col0=col0,
                                          dmask=dmask: e.tensor_tensor(
                                prv[:, :, col0:col0 + 128],
                                prv[:, :, col0:col0 + 128], dmask, op=ALU.mult), cV)
                        lag = (j, col0, exp_done, v_mk2)
                    lj, lcol0, lex, lmk = lag
                    kb.w("tensor", cA, lex)
                    kb.w("tensor", cV, lmk)
                    for h in range(4):
                        pvl = emit_pv(h, lj, lcol0)
                        kb.w("vector", cT, pvl)
                        v_dc = kb.op("vector", lambda e, h=h: e.tensor_copy(
                            denf[:, 512 * h:512 * (h + 1)],
                            psB[:, 512 * h:512 * (h + 1)]), cV)
                    pv_done[lj % 2] = pvl

                    DRS = {0: 32, 1: 0, 2: 96, 3: 0}
                    kb.w("tensor", cV, v_dc)
                    kb.w("tensor", cA, exp_done)
                    for h in range(4):
                        dr = DRS[h]
                        v_rb = kb.op("tensor", lambda e, h=h, dr=dr: e.matmul(
                            ps[0][32 * h:32 * (h + 1), :],
                            onesb[dr:dr + 1, :],
                            denf[dr:dr + 1, 512 * h:512 * (h + 1)],
                            start=True, stop=True,
                            tile_position=(dr, 32 * h)), cT)
                    kb.w("scalar", cT, v_rb)
                    kb.w("scalar", cV, att_tt)
                    kb.op("scalar", lambda e: e.activation(
                        recb[:], ps[0][:, :], AF.Ln, bias=0.0, scale=1.0), cA)
                    exp_done = kb.op("scalar", lambda e: e.activation(
                        recb[:], recb[:], AF.Exp, bias=0.0, scale=-1.0), cA)
                    ex1_done = exp_done
                    kb.w("vector", cA, exp_done)
                    for h in range(4):
                        att_tt = kb.op("vector", lambda e, hg=hg, c=c, h=h: e.tensor_tensor(
                            att[32 * h:32 * (h + 1), 2048 * hg + 512 * c:2048 * hg + 512 * (c + 1)],
                            denf[32 * h:32 * h + 32, 512 * h:512 * (h + 1)],
                            recb[32 * h:32 * (h + 1), :],
                            op=ALU.mult), cV)

            state["core_cT"] = cT.n
            state["core_cV"] = cV.n
            if katt < 7:
                return
            kb.w("tensor", cV, cV.n)
            o_ev = [0, 0]
            for ti in range(NT):
                par = ti % 2
                kb.w("tensor", cV, o_ev[par])
                for f2 in range(2):
                    for kc in range(4):
                        v_o = kb.op("tensor", lambda e, ti=ti, f2=f2, kc=kc, par=par: e.matmul(
                            ps[2 * par + f2][:, :],
                            att[:, 2048 * kc + 128 * ti:2048 * kc + 128 * (ti + 1)],
                            wop_sb[:, D * kc + 512 * f2:D * kc + 512 * (f2 + 1)],
                            start=(kc == 0), stop=(kc == 3)), cT)
                kb.w("vector", cT, v_o)
                for f2 in range(2):
                    v = kb.op("vector", lambda e, ti=ti, f2=f2, par=par: e.tensor_tensor(
                        x_sb[:, D * ti + 512 * f2:D * ti + 512 * (f2 + 1)],
                        x_sb[:, D * ti + 512 * f2:D * ti + 512 * (f2 + 1)],
                        ps[2 * par + f2][:, :], op=ALU.add), cV)
                o_ev[par] = v
                emit_stats(ti)
            state["oproj_cT"] = cT.n
            state["oproj_cV"] = cV.n

        lvl = int(os.environ.get("KSTAGES", "5"))
        nseq = SEQ if lvl >= 4 else 1
        for i in range(NT):
            kb.dma("sync", x_sb[:, D * i:D * (i + 1)],
                   x_d[128 * i:128 * (i + 1), :], dX)
        for s in range(nseq):
            g_list = None
            full = lvl >= 4 or (lvl >= 3 and lvl != 20)
            if lvl >= 1 and lvl != 20:
                kb.scope(f"ffn1_s{s}")
                emit_ffn(s, tail_stats=(lvl >= 2))
                kb.escope()
            if lvl >= 2 or lvl == 20:
                kb.scope(f"attn_s{s}")
                emit_attn(s, skip_stats=(lvl != 20))
                kb.escope()
            if lvl >= 3 and lvl != 20:
                kb.scope(f"ffn2_s{s}")
                g_list = emit_ffn(s, skip_stats=True)
                kb.escope()
            last = (s == nseq - 1)
            for i in range(NT):
                if g_list is not None:
                    kb.w("sync", cV, g_list[i])
                else:
                    kb.w("sync", cV, cV.n)
                kb.dma("sync", out_d[T * s + 128 * i:T * s + 128 * (i + 1), :],
                       x_sb[:, D * i:D * (i + 1)], dSt2[i % 2])
                if g_list is not None:
                    kb.w("tensor", cV, g_list[i])
                    kb.warm(ps[7][0:64, :], wq_sb[:, 0:64], wq_sb[:, 0:512], 2)
                if not last and i >= 1:
                    ii = i - 1
                    kb.w("sync", dSt2[ii % 2], dSt2[ii % 2].n)
                    kb.dma("sync", x_sb[:, D * ii:D * (ii + 1)],
                           x_d[T * (s + 1) + 128 * ii:T * (s + 1) + 128 * (ii + 1), :],
                           dX)
            if not last:
                ii = NT - 1
                kb.w("sync", dSt2[ii % 2], dSt2[ii % 2].n)
                kb.dma("sync", x_sb[:, D * ii:D * (ii + 1)],
                       x_d[T * (s + 1) + 128 * ii:T * (s + 1) + 128 * (ii + 1), :],
                       dX)

        kb.em("sync", lambda e, v=dSt2[0].n: e.wait_ge(dSt2[0].h, v))
        kb.em("sync", lambda e, v=dSt2[1].n: e.wait_ge(dSt2[1].h, v))

        with nc.Block() as block:
            block.tensor(lambda eng: [f(eng) for f in kb.th["tensor"]])
            block.vector(lambda eng: [f(eng) for f in kb.th["vector"]])
            block.scalar(lambda eng: [f(eng) for f in kb.th["scalar"]])
            block.gpsimd(lambda eng: [f(eng) for f in kb.th["gpsimd"]])
            block.sync(lambda eng: [f(eng) for f in kb.th["sync"]])

    return nc


def _consts():
    p = np.arange(128)[:, None]
    c128 = np.arange(128)[None, :]
    cf = np.zeros((128, CF), np.float32)
    cf[:, C_IDENT:C_IDENT + 128] = (p == c128)
    cf[:, C_L128:C_L128 + 128] = (p < c128)
    cf[0:16, C_L16:C_L16 + 16] = (np.arange(16)[:, None] < np.arange(16)[None, :])
    cf[:, C_IOTA8:C_IOTA8 + 8] = np.arange(8)[None, :]
    cf[:, C_SELQ:C_SELQ + 2] = (p // 64 == np.arange(2)[None, :])
    cf[:, C_SELK:C_SELK + 4] = (p // 32 == np.arange(4)[None, :])
    cf[0:2, C_BCQ:C_BCQ + 128] = (c128 // 64 == np.arange(2)[:, None])
    cf[0:4, C_BCK:C_BCK + 128] = (c128 // 32 == np.arange(4)[:, None])
    cf[0:1, C_ONESROW:C_ONESROW + 128] = 1.0
    cf[:, C_ONESCOL] = 1.0
    cf[:, C_TRASH] = float(ZROW)
    cf[:, C_ONESB:C_ONESB + 32] = 1.0
    cf[:, C_IMM:C_IMM + len(IMM_VALS)] = np.array(IMM_VALS, np.float32)[None, :]
    cb = np.zeros((128, CB_COLS), np.float32)
    cb[:, 0:896] = (np.arange(896)[None, :] - 384 - p) >= 0
    cb[:, 896] = 1.0
    return cf, cb


def kernel(L0_attn_x, L0_attn_Wq, L0_attn_Wkv, L0_attn_Wk, L0_attn_Wv,
           L0_attn_Wo, L0_attn_q_g, L0_attn_k_g, L0_ffn_Wr, L0_ffn_We):
    x = np.ascontiguousarray(np.asarray(L0_attn_x, np.float32))
    B = x.shape[0]
    wo = np.asarray(L0_attn_Wo, np.float32)
    ridx = np.zeros(H * ROPE, np.int64)
    for g in range(4):
        for j in range(4):
            for dv in range(ROPE):
                ridx[128 * g + 32 * j + dv] = 64 * (4 * g + j) + dv
    wop = np.ascontiguousarray(wo[ridx])
    wrt = np.ascontiguousarray(np.asarray(L0_ffn_Wr, np.float32).T)
    cperm = []
    for half in range(2):
        for h in range(H):
            cperm.extend(range(64 * h + 32 * half, 64 * h + 32 * half + 32))
    wq_p = np.ascontiguousarray(np.asarray(L0_attn_Wq, np.float32)[:, cperm])
    cf, cb = _consts()

    common = dict(
        wq=wq_p,
        wkv=np.ascontiguousarray(np.asarray(L0_attn_Wkv, np.float32)),
        wk=np.ascontiguousarray(np.asarray(L0_attn_Wk, np.float32)),
        wv=np.ascontiguousarray(np.asarray(L0_attn_Wv, np.float32)),
        wop=wop, wrt=wrt,
        we=np.ascontiguousarray(np.asarray(L0_ffn_We, np.float32)),
        qg=np.ascontiguousarray(np.asarray(L0_attn_q_g, np.float32)),
        kg=np.ascontiguousarray(np.asarray(L0_attn_k_g, np.float32)),
        qgc=np.ascontiguousarray(
            np.tile(np.asarray(L0_attn_q_g, np.float32)[:ROPE], 4)[:, None]),
        kgc=np.ascontiguousarray(
            np.tile(np.asarray(L0_attn_k_g, np.float32), 4)[:, None]),
        cstf=cf, cstb=cb,
    )
    in_maps = []
    for core in range(NCORES):
        m = dict(common)
        m["x"] = np.ascontiguousarray(
            x[SEQ * core:SEQ * (core + 1)].reshape(SEQ * T, D))
        in_maps.append(m)

    nc = build_program()
    res = run_bass_kernel_spmd(nc, in_maps, core_ids=list(range(NCORES)))
    out = np.empty((B, T, D), np.float32)
    for core in range(NCORES):
        out[SEQ * core:SEQ * (core + 1)] = (
            res.results[core]["out_x"].reshape(SEQ, T, D))
    return out
